# version 05
# baseline (speedup 1.0000x reference)
import contextlib
import math
import numpy as np
import ml_dtypes
import concourse.bass as bass
import concourse.mybir as mybir
from concourse.bass_utils import run_bass_kernel_spmd

F32 = mybir.dt.float32
BF16 = mybir.dt.bfloat16
AF = mybir.ActivationFunctionType
ALU = mybir.AluOpType
AX = mybir.AxisListType

T = 4096
D = 1024
NCH = 8
TT = 512
NTT = T // TT
CW = T + 1
NCORES = 8
NEG = -1.0e30
EPS_LN = 1e-5

BF = ml_dtypes.bfloat16


def _host_constants():
    c = {}
    c["ident"] = np.eye(128, dtype=BF)
    c["ident_f32"] = np.eye(128, dtype=np.float32)
    c["ones_col_bf"] = np.ones((128, 1), dtype=BF)
    c["ones_row_f32"] = np.ones((1, 128), dtype=np.float32)
    c["ones_row_bf"] = np.ones((1, 128), dtype=BF)

    dm = np.zeros((4, 128, 512), dtype=np.float32)
    r = np.arange(128)[:, None]
    j = np.arange(512)[None, :]
    for p in range(4):
        dm[p] = np.where(j <= p * 128 + r, 0.0, NEG)
    c["diagmask"] = dm.astype(BF)

    lm = np.zeros((2, 128, 512), dtype=np.float32)
    j2 = np.arange(512)[None, :]
    for p in range(2):
        i = p * 128 + r
        lm[p] = np.where((j2 >= i) & (j2 <= i + 256), 0.0, NEG)
    c["locmask"] = lm.astype(BF)
    lm0 = np.zeros((2, 128, 256), dtype=np.float32)
    j3 = np.arange(256)[None, :]
    for p in range(2):
        lm0[p] = np.where(j3 <= p * 128 + r, 0.0, NEG)
    c["locmask0"] = lm0.astype(BF)

    sgn = np.ones((64, 1))
    sgn[0:32] = -1.0
    invf = (1.0 / 10000.0 ** (np.arange(0, 64, 2, dtype=np.float32) / 64.0)).astype(np.float64)
    pos = np.arange(T, dtype=np.float64)
    ang = pos[None, :] * invf[(np.arange(64) % 32)][:, None]
    c["l0_cos"] = np.cos(ang).astype(BF)
    c["l0_sin"] = (np.sin(ang) * sgn).astype(BF)

    half = np.arange(0, 64, 2, dtype=np.float64)
    invf2 = 1.0 / 10000.0 ** (half / 64.0)
    fr = pos[None, :] * invf2[(np.arange(64) % 32)][:, None]
    base = (half + 0.4 * 64.0) / (1.4 * 64.0)
    power = (pos - T // 2) / (256 // 2)
    scale = base[(np.arange(64) % 32)][:, None] ** power[None, :]
    c["xp_cq"] = (np.cos(fr) * scale).astype(BF)
    c["xp_sq"] = (np.sin(fr) * scale * sgn).astype(BF)
    c["xp_ck"] = (np.cos(fr) / scale).astype(BF)
    c["xp_sk"] = (np.sin(fr) / scale * sgn).astype(BF)

    sel = np.zeros((128, 2), dtype=BF)
    sel[0:64, 0] = 1
    sel[64:128, 1] = 1
    c["sel_ssq"] = sel
    selb = np.zeros((2, 128), dtype=BF)
    selb[0, 0:64] = 1
    selb[1, 64:128] = 1
    c["sel_bcast2"] = selb
    s8 = np.zeros((8, 8 * 128), dtype=BF)
    for e in range(8):
        s8[e, e * 128:(e + 1) * 128] = 1
    c["sel_bcast8"] = s8
    return c


def _l0_perm():
    p = np.zeros(D, dtype=np.int64)
    for h in range(16):
        for jj in range(64):
            src = h * 64 + (2 * jj if jj < 32 else 2 * (jj - 32) + 1)
            p[h * 64 + jj] = src
    return p


def _l1_qkv_perm():
    return np.arange(3072, dtype=np.int64)


def _prep_weights(inputs):
    w = {}
    perm = _l0_perm()
    w["l0_wq"] = np.ascontiguousarray(inputs["L0_attn_Wq"][:, perm])
    w["l0_wk"] = np.ascontiguousarray(inputs["L0_attn_Wk"][:, perm])
    w["l0_wv"] = np.ascontiguousarray(inputs["L0_attn_Wv"])
    w["l0_wo"] = np.ascontiguousarray(inputs["L0_attn_Wo"])
    w["l0_bo"] = np.ascontiguousarray(inputs["L0_attn_bo"].reshape(D, 1))
    w["moe_wg"] = np.ascontiguousarray(inputs["L0_ffn_w_gate"].T)
    w["moe_w1"] = np.ascontiguousarray(inputs["L0_ffn_W1"])
    w["moe_b1"] = np.ascontiguousarray(
        inputs["L0_ffn_b1"].reshape(8, 32, 128).transpose(0, 2, 1))
    w["moe_w2"] = np.ascontiguousarray(inputs["L0_ffn_W2"])
    w["moe_b2"] = np.ascontiguousarray(inputs["L0_ffn_b2"])
    w["mla_down"] = np.ascontiguousarray(inputs["L1_attn_w_kv_down"])
    w["mla_up"] = np.ascontiguousarray(inputs["L1_attn_w_kv_up"])
    w["mla_wq"] = np.ascontiguousarray(inputs["L1_attn_w_q"])
    w["mla_wo"] = np.ascontiguousarray(inputs["L1_attn_w_out"])

    for i in range(6):
        w[f"lf_wqkv_{i}"] = np.ascontiguousarray(inputs["L1_ffn_Wqkv"][i])
        w[f"lf_qs_{i}"] = np.ascontiguousarray(inputs["L1_ffn_q_scale"][i])
        w[f"lf_ks_{i}"] = np.ascontiguousarray(inputs["L1_ffn_k_scale"][i])
        w[f"lf_wo_{i}"] = np.ascontiguousarray(inputs["L1_ffn_Wo"][i])
    DI = 2730
    DIP = 2816
    W1 = inputs["L1_ffn_W1"]
    b1 = inputs["L1_ffn_b1"]
    W2 = inputs["L1_ffn_W2"]
    w1p = np.zeros((6, D, 2 * DIP), dtype=np.float32)
    b1p = np.zeros((6, 2 * DIP), dtype=np.float32)
    w2p = np.zeros((6, DIP, D), dtype=np.float32)
    for kk in range(DIP // 128):
        lo, hi = kk * 128, min((kk + 1) * 128, DI)
        n = hi - lo
        if n > 0:
            w1p[:, :, 2 * kk * 128: 2 * kk * 128 + n] = W1[:, :, lo:hi]
            w1p[:, :, (2 * kk + 1) * 128: (2 * kk + 1) * 128 + n] = W1[:, :, DI + lo: DI + hi]
            b1p[:, 2 * kk * 128: 2 * kk * 128 + n] = b1[:, lo:hi]
            b1p[:, (2 * kk + 1) * 128: (2 * kk + 1) * 128 + n] = b1[:, DI + lo: DI + hi]
            w2p[:, lo:hi, :] = W2[:, lo:hi, :]
    b1c = np.ascontiguousarray(b1p.reshape(6, 44, 128).transpose(0, 2, 1))
    gc = np.ascontiguousarray(inputs["L1_ffn_ln_g"].reshape(6, 8, 128).transpose(0, 2, 1))
    bc = np.ascontiguousarray(inputs["L1_ffn_ln_b"].reshape(6, 8, 128).transpose(0, 2, 1))
    b2c = np.ascontiguousarray(inputs["L1_ffn_b2"].reshape(6, 8, 128).transpose(0, 2, 1))
    for i in range(6):
        w[f"lf_w1_{i}"] = np.ascontiguousarray(w1p[i])
        w[f"lf_w2_{i}"] = np.ascontiguousarray(w2p[i])
        w[f"lf_b1c_{i}"] = np.ascontiguousarray(b1c[i])
        w[f"lf_gc_{i}"] = np.ascontiguousarray(gc[i])
        w[f"lf_bc_{i}"] = np.ascontiguousarray(bc[i])
        w[f"lf_b2c_{i}"] = np.ascontiguousarray(b2c[i])
    return w



ENGS = ("sync", "tensor", "vector", "scalar", "gpsimd")
N_DMA_SEMS = 24


class Prog:
    def __init__(self, nc, stack):
        self.nc = nc
        self.stack = stack
        self.gstack = stack
        self.streams = {e: [] for e in ENGS}
        self.cnt = {}
        self.sems = {}
        self.waited = {e: {} for e in ENGS}
        self.dma_rr = 0
        for name in ("Ete", "Eve", "Ese", "Egp", "Esy", "dout"):
            self._mksem(name)
        for i in range(N_DMA_SEMS):
            self._mksem(f"d{i}")
        for i in range(8):
            self._mksem(f"g{i}")
        for i in range(6):
            self._mksem(f"gl{i}")
        self.gdma_rr = 0
        self.glrr = 0
        self.consts = {}
        self.dram = {}
        self._constf = {}
        self.last_tok = {e: None for e in ENGS}

    def group_sem(self):
        s = f"gl{self.glrr % 6}"
        self.glrr += 1
        return s

    def constcol(self, value, dtype=F32):
        key = (dtype, float(value))
        if key not in self._constf:
            assert self.stack is self.gstack, (
                f"constcol({value}) first used inside a phase scope; "
                f"pre-register it in ph_const_load")
            npdt = np.float32 if dtype == F32 else BF
            data = np.full((128, 1), value, dtype=npdt)
            dh = self.nc.inline_tensor(data, f"cf{len(self._constf)}")
            t = self.gstack.enter_context(
                self.nc.sbuf_tensor(f"cfs{len(self._constf)}", [128, 1], dtype))
            tok = self.dma("sync", t[:], dh.ap())
            self._constf[key] = (t, tok)
        return self._constf[key]

    def _mksem(self, name):
        self.sems[name] = self.stack.enter_context(self.nc.semaphore(name))
        self.cnt[name] = 0

    def sb(self, name, shape, dtype):
        self._uid = getattr(self, "_uid", 0) + 1
        return self.stack.enter_context(
            self.nc.sbuf_tensor(f"{name}_u{self._uid}", shape, dtype))

    def ps(self, name, shape, dtype=F32):
        self._uid = getattr(self, "_uid", 0) + 1
        return self.stack.enter_context(
            self.nc.psum_tensor(f"{name}_u{self._uid}", shape, dtype))

    @contextlib.contextmanager
    def scope(self):
        with contextlib.ExitStack() as st:
            old = self.stack
            self.stack = st
            try:
                yield
            finally:
                self.stack = old

    def q(self, eng, fn, wait=(), inc=None, incn=1):
        wait = list(wait)
        if inc is None and eng in ("vector", "scalar", "gpsimd"):
            inc = {"vector": "Eve", "scalar": "Ese", "gpsimd": "Egp"}[eng]
        if inc is not None and self.cnt[inc] > 0:
            wait.append((inc, self.cnt[inc]))
        w = []
        for s, v in wait:
            if v <= 0:
                continue
            if self.waited[eng].get(s, 0) >= v:
                continue
            self.waited[eng][s] = v
            w.append((s, v))
        tok = None
        if inc is not None:
            self.cnt[inc] += incn
            tok = (inc, self.cnt[inc])
            self.last_tok[eng] = tok
        self.streams[eng].append((fn, tuple(w), inc, incn))
        return tok

    def te(self, fn, **kw):
        return self.q("tensor", fn, **kw)

    def ve(self, fn, **kw):
        return self.q("vector", fn, **kw)

    def se(self, fn, **kw):
        return self.q("scalar", fn, **kw)

    def gp(self, fn, **kw):
        return self.q("gpsimd", fn, **kw)

    def sy(self, fn, **kw):
        return self.q("sync", fn, **kw)

    def dma(self, queue, out_ap, in_ap, wait=(), sem=None, **dkw):
        if sem is None:
            if queue == "gpsimd":
                sem = f"g{self.gdma_rr % 8}"
                self.gdma_rr += 1
            else:
                sem = f"d{self.dma_rr % N_DMA_SEMS}"
                self.dma_rr += 1
        h = self.sems[sem]
        if queue == "gpsimd" and self.cnt[sem] > 0:
            wait = list(wait) + [(sem, self.cnt[sem])]
        def fn(e, o=out_ap, i=in_ap, kw2=dkw):
            return e.dma_start(out=o, in_=i, **kw2)
        return self.q(queue, fn, wait=wait, inc=sem, incn=16)

    def val(self, sem):
        return self.cnt[sem]

    def barrier(self):
        vals = [(s, self.cnt[s]) for s in self.sems if self.cnt[s] > 0]
        own = {"sync": "Esy", "tensor": "Ete", "vector": "Eve", "scalar": "Ese",
               "gpsimd": "Egp"}
        toks = {}
        for e in ENGS:
            lw = [self.last_tok[e]] if self.last_tok[e] else []
            toks[e] = self.q(e, lambda en: en.nop(), wait=lw, inc=own[e])
        for e in ENGS:
            w = [t for ee, t in toks.items() if ee != e] + vals
            self.q(e, lambda en: en.nop(), wait=w)

    def emit(self, which, e):
        import os
        noattach = bool(os.environ.get("KNOATTACH")) or which not in (
            "tensor", "sync")
        for fn, waits, inc, incn in self.streams[which]:
            head = waits if noattach else waits[:-1]
            for s, v in head:
                e.wait_ge(self.sems[s], v)
            inst = fn(e)
            if inst is None and (waits or inc is not None):
                inst = e.nop()
            if waits and inst is not None and not noattach:
                s, v = waits[-1]
                inst._wait_ge(self.sems[s], v)
            if inc is not None:
                inst.then_inc(self.sems[inc], incn)



def ph_const_load(P):
    nc = P.nc
    C = _host_constants()
    specs = {
        "ident": ([128, 128], BF16),
        "ones_col_bf": ([128, 1], BF16),
        "ones_row_f32": ([1, 128], F32),
        "ones_row_bf": ([1, 128], BF16),
        "sel_ssq": ([128, 2], BF16),
        "sel_bcast2": ([2, 128], BF16),
        "sel_bcast8": ([8, 8 * 128], BF16),
        "ident_f32": ([128, 128], F32),
    }
    flat = {
        "diagmask": C["diagmask"].transpose(1, 0, 2).reshape(128, 4 * 512),
        "locmask": C["locmask"].transpose(1, 0, 2).reshape(128, 2 * 512),
        "locmask0": C["locmask0"].transpose(1, 0, 2).reshape(128, 2 * 256),
    }
    for name, (shape, dt) in specs.items():
        data = flat.get(name, C[name])
        dh = nc.inline_tensor(np.ascontiguousarray(data), f"c_{name}")
        t = P.sb(f"sb_{name}", shape, dt)
        P.consts[name] = t
        P.dma("sync", t[:], dh.ap())
    for name in ("l0_cos", "l0_sin", "xp_cq", "xp_sq", "xp_ck", "xp_sk"):
        P.dram[name] = nc.inline_tensor(np.ascontiguousarray(C[name]), f"c_{name}")
    for name in ("diagmask", "locmask", "locmask0"):
        P.dram[name] = nc.inline_tensor(np.ascontiguousarray(flat[name]), f"c_{name}")
    for v in (EPS_LN, 0.125, 8.0, 1e-12, 1.0, -0.5, 1e-24):
        P.constcol(v)
    P.barrier()


def ph_load_x(P, src_name="x_fm"):
    x = P.dram[src_name]
    R = P.R
    for c in range(NCH):
        P.ve(lambda e, c=c: e.memset(R[:, c * CW: c * CW + 1], 0.0))
        P.dma("sync", R[:, c * CW + 1: (c + 1) * CW], x.ap()[c * 128:(c + 1) * 128, :])
    P.barrier()


def ph_store_out(P, dst="out"):
    o = P.dram[dst]
    R = P.R
    toks = []
    for c in range(NCH):
        toks.append(P.dma("sync", o.ap()[c * 128:(c + 1) * 128, :],
                          R[:, c * CW + 1: (c + 1) * CW], sem="dout"))
    P.sy(lambda e: e.nop(), wait=[toks[-1]])


def ph_store_R_dbg(P, dst):
    ph_store_out(P, dst)
    P.barrier()


def emit_ln_inplace(P):
    nc = P.nc
    R = P.R
    ones_bf = P.consts["ones_col_bf"]
    ones_row = P.consts["ones_row_f32"]
    with P.scope():
        xb = [P.sb(f"ln_xb{i}", [128, TT], BF16) for i in range(2)]
        sq = [P.sb(f"ln_sq{i}", [128, TT], BF16) for i in range(2)]
        rowbuf = P.sb("ln_rows", [1, 6 * TT], F32)
        sP = P.ps("ln_s", [1, TT])
        qP = P.ps("ln_q", [1, TT])
        rbP = P.ps("ln_rb", [128, TT])
        mbP = P.ps("ln_mb", [128, TT])
        mean = rowbuf[:, 0 * TT:1 * TT]
        ex2 = rowbuf[:, 1 * TT:2 * TT]
        var = rowbuf[:, 2 * TT:3 * TT]
        sd = rowbuf[:, 3 * TT:4 * TT]
        rstd = rowbuf[:, 4 * TT:5 * TT]
        mr = rowbuf[:, 5 * TT:6 * TT]

        mm_toks = [None, None]
        for tt in range(NTT):
            base = 1 + tt * TT
            for c in range(NCH):
                sl = R[:, c * CW + base: c * CW + base + TT]
                par = c % 2
                guard = [mm_toks[par]] if mm_toks[par] else []
                tcast = P.ve(lambda e, sl=sl, b=xb[par]: e.tensor_copy(b[:], sl),
                             wait=guard, inc="Eve")
                tsq = P.se(lambda e, sl=sl, b=sq[par]: e.activation(b[:], sl, AF.Square),
                           wait=guard, inc="Ese")
                P.te(lambda e, b=xb[par], c=c: e.matmul(
                    sP[:], ones_bf[:], b[:], start=(c == 0), stop=(c == NCH - 1)),
                    wait=[tcast])
                tk = P.te(lambda e, b=sq[par], c=c: e.matmul(
                    qP[:], ones_bf[:], b[:], start=(c == 0), stop=(c == NCH - 1)),
                    wait=[tsq], inc="Ete")
                mm_toks[par] = tk
                if c == NCH - 1:
                    mm_done = tk
            t1 = P.ve(lambda e: e.tensor_scalar_mul(mean, sP[:], 1.0 / D),
                      wait=[mm_done], inc="Eve")
            t2 = P.ve(lambda e: e.tensor_scalar_mul(ex2, qP[:], 1.0 / D), inc="Eve")
            t3 = P.ve(lambda e: e.tensor_mul(var, mean, mean), wait=[t1, t2], inc="Eve")
            tv = P.ve(lambda e: e.tensor_sub(var, ex2, var), wait=[t3], inc="Eve")
            epst, epstok = P.constcol(EPS_LN)
            nhalf, nhtok = P.constcol(-0.5)
            tln = P.se(lambda e: e.activation(sd, var, AF.Ln, bias=epst[0:1, :]),
                       wait=[tv, epstok, nhtok], inc="Ese")
            ts = P.se(lambda e: e.activation(rstd, sd, AF.Exp, scale=nhalf[0:1, :]),
                      wait=[tln], inc="Ese")
            tmr = P.ve(lambda e: e.tensor_mul(mr, mean, rstd), wait=[ts], inc="Eve")
            tb1 = P.te(lambda e: e.matmul(rbP[:], ones_row[:], rstd, start=True,
                                          stop=True), wait=[tmr], inc="Ete")
            tb2 = P.te(lambda e: e.matmul(mbP[:], ones_row[:], mr, start=True,
                                          stop=True), inc="Ete")
            for c in range(NCH):
                sl = R[:, c * CW + base: c * CW + base + TT]
                P.ve(lambda e, sl=sl: e.tensor_mul(sl, sl, rbP[:]), wait=[tb1, tb2])
                tlast = P.ve(lambda e, sl=sl: e.tensor_sub(sl, sl, mbP[:]),
                             inc="Eve" if c == NCH - 1 else None)
            P.te(lambda e: e.nop(), wait=[tlast])
    P.barrier()



def _load_w_sbuf(P, wsb, dram_name, rows, cols, idx=None):
    W = P.dram[dram_name].ap()
    if idx is not None:
        W = W[idx]
    sem = P.group_sem()
    tok = None
    for rc in range(rows // 128):
        tok = P.dma("gpsimd", wsb[:, rc * cols:(rc + 1) * cols],
                    W[rc * 128:(rc + 1) * 128, :], sem=sem)
    return [tok]


def _wslice(wsb, cols, rc, oc, width=128):
    return wsb[:, rc * cols + oc * width: rc * cols + oc * width + width]


def ph_l0_proj(P):
    R = P.R
    with P.scope():
        wq = P.sb("wq", [128, 8 * D], BF16)
        wk = P.sb("wk", [128, 8 * D], BF16)
        wv = P.sb("wv", [128, 8 * D], BF16)
        wtoks = (_load_w_sbuf(P, wq, "l0_wq", D, D) + _load_w_sbuf(P, wk, "l0_wk", D, D)
                 + _load_w_sbuf(P, wv, "l0_wv", D, D))
        emit_qkv_from_R(P, wq=wq, wk=wk, wv=wv, wtoks=wtoks, shift=False)
    P.barrier()


def emit_qkv_from_R(P, wq, wk, wv, wtoks, shift=False, qs_b=None, ks_b=None,
                    l2norm=False, rq_tm=None):
    R = P.R
    xst = [P.sb(f"pj_x{i}", [128, TT], BF16) for i in range(8)]
    ev = [P.sb(f"pj_e{i}", [128, TT], BF16) for i in range(4)]
    pq = [P.ps(f"pj_p{i}", [128, TT]) for i in range(3)]
    q_sp, k_sp, v_sp = P.dram["q_sp"], P.dram["k_sp"], P.dram["v_sp"]
    last_mm = [None] * 8
    ev_tok = [None] * 4
    pq_tok = [None] * 3
    pqi = 0
    evi = 0
    for tt in range(NTT):
        base = 1 + tt * TT
        casts = []
        for c in range(NCH):
            off = 0 if (shift and c >= NCH // 2) else 1
            sl = R[:, c * CW + base - 1 + off: c * CW + base - 1 + off + TT]
            g = [last_mm[c]] if last_mm[c] else []
            casts.append(P.ve(lambda e, sl=sl, b=xst[c]: e.tensor_copy(b[:], sl),
                              wait=g, inc="Eve"))
        allcast = casts[-1]
        for which, wsb, sp in (("q", wq, q_sp), ("k", wk, k_sp)):
            for oc in range(8):
                pp = pq[pqi % 3]
                gw = [pq_tok[pqi % 3]] if pq_tok[pqi % 3] else []
                for dc in range(NCH):
                    tk = P.te(lambda e, pp=pp, w=_wslice(wsb, D, dc, oc),
                              b=xst[dc], dc=dc: e.matmul(
                                  pp[:], w, b[:], start=(dc == 0), stop=(dc == NCH - 1)),
                              wait=([casts[dc]] + wtoks + gw) if dc == 0 else [casts[dc]],
                              inc="Ete" if dc == NCH - 1 else None)
                mmdone = tk
                last_mm_update = tk
                for dc in range(NCH):
                    last_mm[dc] = tk
                eb = ev[evi % 4]
                ge = [ev_tok[evi % 4]] if ev_tok[evi % 4] else []
                tev = P.se(lambda e, pp=pp, b=eb: e.copy(b[:], pp[:]),
                           wait=[mmdone] + ge, inc="Ese")
                pq_tok[pqi % 3] = tev
                ev_tok[evi % 4] = P.dma(
                    "sync", sp.ap()[oc * 128:(oc + 1) * 128, tt * TT:(tt + 1) * TT],
                    eb[:], wait=[tev])
                pqi += 1
                evi += 1
        for tc in range(4):
            for half in range(2):
                pp = pq[pqi % 3]
                gw = [pq_tok[pqi % 3]] if pq_tok[pqi % 3] else []
                for dc in range(NCH):
                    tk = P.te(lambda e, pp=pp, lh=xst[dc], tc=tc, half=half, dc=dc:
                              e.matmul(pp[:], lh[:, tc * 128:(tc + 1) * 128],
                                       wv[:, dc * D + half * TT: dc * D + half * TT + TT],
                                       start=(dc == 0), stop=(dc == NCH - 1)),
                              wait=([allcast] + gw) if dc == 0 else [],
                              inc="Ete" if dc == NCH - 1 else None)
                mmdone = tk
                for dc in range(NCH):
                    last_mm[dc] = tk
                eb = ev[evi % 4]
                ge = [ev_tok[evi % 4]] if ev_tok[evi % 4] else []
                tev = P.se(lambda e, pp=pp, b=eb: e.copy(b[:], pp[:]),
                           wait=[mmdone] + ge, inc="Ese")
                pq_tok[pqi % 3] = tev
                ev_tok[evi % 4] = P.dma(
                    "sync",
                    v_sp.ap()[tt * 4 + tc, :, half * TT:(half + 1) * TT],
                    eb[:], wait=[tev])
                pqi += 1
                evi += 1


def emit_causal_attention(P, rope=False, scale=0.125):
    ST = 2048
    with P.scope():
        vq = P.sb("at_vq", [128, 32 * 128], BF16)
        qh = P.sb("at_qh", [64, T], BF16)
        kh = P.sb("at_kh", [64, T], BF16)
        Pb = P.sb("at_P", [128, T], BF16)
        oh = P.sb("at_oh", [64, T], BF16)
        pts = P.sb("at_pts", [128, TT], BF16)
        cols = P.sb("at_cols", [128, 8], F32)
        ctab = P.sb("at_ctab", [64, T], BF16) if rope else None
        stab = P.sb("at_stab", [64, T], BF16) if rope else None
        swld = P.sb("at_swld", [64, T], BF16) if rope else None
        S = P.ps("at_S", [128, ST])
        PT = P.ps("at_PT", [128, TT], BF16)
        OT = P.ps("at_OT", [64, 128])
        ident = P.consts["ident"]
        dmask = P.sb("at_dmask", [128, 4 * 512], BF16)
        dm_tok = P.dma("sync", dmask[:], P.dram["diagmask"].ap())
        sc_col, sc_tok = P.constcol(scale)
        q_sp, k_sp, v_sp, o_sp = (P.dram[n] for n in ("q_sp", "k_sp", "v_sp", "o_sp"))
        mneg0 = cols[:, 0:1]
        mneg1 = cols[:, 1:2]
        bias = cols[:, 2:3]
        l0c = cols[:, 3:4]
        l1c = cols[:, 4:5]
        fc = cols[:, 5:6]
        rc_ = cols[:, 6:7]
        tmpc = cols[:, 7:8]

        if rope:
            ttok = [P.dma("sync", ctab[:], P.dram["l0_cos"].ap()),
                    P.dma("sync", stab[:], P.dram["l0_sin"].ap())]
        else:
            ttok = []

        vq_tok = None
        qk_free = None
        oh_tok = None
        prev_exp = None
        prev_oev = None
        prev_av = None
        pt_guard = None
        for h in range(16):
            if h % 2 == 0:
                gw = [t for t in (prev_av,) if t]
                sem = P.group_sem()
                for j in range(32):
                    vt = P.dma(
                        "sync", vq[:, j * 128:(j + 1) * 128,],
                        P.dram["v_sp"].ap()[j, :, (h // 2) * 128:(h // 2 + 1) * 128],
                        wait=gw if j == 0 else [], sem=sem)
                vq_toks = [vt]
            gw = [t for t in (qk_free,) if t]
            tq = P.dma("sync", qh[:], q_sp.ap()[h * 64:(h + 1) * 64, :], wait=gw)
            tk_ = P.dma("sync", kh[:], k_sp.ap()[h * 64:(h + 1) * 64, :], wait=gw)
            rope_done = []
            if rope:
                swfree = None
                for src, sp_, ldt in ((qh, q_sp, tq), (kh, k_sp, tk_)):
                    T1 = Pb[0:64, :]
                    ds1 = P.dma("sync", swld[0:32, :],
                                sp_.ap()[h * 64 + 32: h * 64 + 64, :],
                                wait=[swfree] if swfree else [])
                    ds2 = P.dma("sync", swld[32:64, :],
                                sp_.ap()[h * 64: h * 64 + 32, :])
                    a1 = P.ve(lambda e, s=src: e.tensor_mul(T1, s[:], ctab[:]),
                              wait=[ldt] + ttok + [t for t in (prev_av,) if t]
                              + rope_done[-1:], inc="Eve")
                    a2 = P.ve(lambda e, s=src: e.tensor_mul(s[:], swld[:], stab[:]),
                              wait=[a1, ds1, ds2], inc="Eve")
                    swfree = a2
                    rope_done.append(P.ve(lambda e, s=src: e.tensor_add(
                        s[:], s[:], T1), wait=[a2], inc="Eve"))
                ldw = list(rope_done)
            else:
                ldw = [tq, tk_]

            pnorm_tok = None
            for c in range(32):
                klen = (c + 1) * 128
                ns = (klen + ST - 1) // ST
                for s in range(ns):
                    w = min(ST, klen - s * ST)
                    ntile = (w + TT - 1) // TT
                    for kt in range(ntile):
                        n = min(TT, w - kt * TT)
                        isdiag = (s == ns - 1 and kt == ntile - 1)
                        islastmm = (kt == ntile - 1) and not isdiag
                        tkk = P.te(lambda e, c=c, s=s, kt=kt, n=n, isdiag=isdiag:
                                   e.matmul(
                                       S[:, kt * TT: kt * TT + n],
                                       qh[:, c * 128:(c + 1) * 128],
                                       kh[:, s * ST + kt * TT: s * ST + kt * TT + n],
                                       start=True, stop=not isdiag),
                                   wait=(ldw + [t for t in (prev_exp,) if t])
                                   if kt == 0 else [],
                                   inc="Ete" if islastmm else None)
                        if isdiag:
                            tkk = P.te(lambda e, c=c, kt=kt, n=n: e.matmul(
                                S[:, kt * TT: kt * TT + n], ident[:],
                                dmask[:, (c % 4) * TT: (c % 4) * TT + n],
                                start=False, stop=True),
                                wait=[dm_tok], inc="Ete")
                    qk_done = tkk
                    mn = mneg0 if s == 0 else mneg1
                    tm = P.ve(lambda e, mn=mn, w=w: e.reduce_max(
                        mn, S[:, 0:w], axis=AX.X, negate=True),
                        wait=[qk_done], inc="Eve")
                    if s == 0:
                        tb = P.ve(lambda e: e.tensor_mul(bias, mneg0, sc_col[:]),
                                  wait=[tm, sc_tok], inc="Eve")
                        lcur = l0c
                    else:
                        t1 = P.ve(lambda e: e.tensor_tensor(
                            tmpc, mneg0, mneg1, ALU.min), wait=[tm], inc="Eve")
                        td = P.ve(lambda e: e.tensor_sub(fc, tmpc, mneg0),
                                  wait=[t1], inc="Eve")
                        td2 = P.ve(lambda e: e.tensor_mul(fc, fc, sc_col[:]),
                                   wait=[td], inc="Eve")
                        tb = P.ve(lambda e: e.tensor_mul(bias, tmpc, sc_col[:]),
                                  wait=[t1], inc="Eve")
                        tf = P.se(lambda e: e.activation(fc, fc, AF.Exp),
                                  wait=[td2], inc="Ese")
                        lcur = l1c
                    texp = P.se(lambda e, s=s, w=w, lc=lcur: e.activation(
                        Pb[:, s * ST: s * ST + w], S[:, 0:w], AF.Exp,
                        bias=bias, scale=sc_col[:], accum_out=lc),
                        wait=[tb], inc="Ese")
                    prev_exp = texp
                if ns == 1:
                    tr0 = P.ve(lambda e: e.reciprocal(rc_, l0c), wait=[prev_exp],
                               inc="Eve")
                    tnorm = P.ve(lambda e, klen=klen: e.tensor_scalar_mul(
                        Pb[:, 0:klen], Pb[:, 0:klen], rc_), wait=[tr0], inc="Eve")
                else:
                    tp0 = P.ve(lambda e: e.tensor_scalar_mul(
                        Pb[:, 0:ST], Pb[:, 0:ST], fc), wait=[tf, prev_exp], inc="Eve")
                    tl0 = P.ve(lambda e: e.tensor_mul(l0c, l0c, fc), wait=[tf, prev_exp],
                               inc="Eve")
                    tl1 = P.ve(lambda e: e.tensor_add(l0c, l0c, l1c), wait=[tl0],
                               inc="Eve")
                    tr0 = P.ve(lambda e: e.reciprocal(rc_, l0c), wait=[tl1], inc="Eve")
                    tnorm = P.ve(lambda e, klen=klen: e.tensor_scalar_mul(
                        Pb[:, 0:klen], Pb[:, 0:klen], rc_), wait=[tr0, tp0],
                        inc="Eve")
                pnorm_tok = tnorm
                ngr = (klen + TT - 1) // TT
                for g in range(ngr):
                    gw2 = min(TT, klen - g * TT)
                    nt = gw2 // 128
                    for jj in range(nt):
                        tpt = P.te(lambda e, g=g, jj=jj: e.transpose(
                            PT[:, jj * 128:(jj + 1) * 128],
                            Pb[:, g * TT + jj * 128: g * TT + (jj + 1) * 128],
                            ident[:]),
                            wait=([tnorm, pt_guard] if jj == 0 and pt_guard
                                  else ([tnorm] if jj == 0 else [])),
                            inc="Ete" if jj == nt - 1 else None)
                    tev = P.se(lambda e, gw2=gw2: e.copy(pts[:, 0:gw2], PT[:, 0:gw2]),
                               wait=[tpt], inc="Ese")
                    pt_guard = tev
                    for jj in range(nt):
                        gchunk = g * 4 + jj
                        first = (g == 0 and jj == 0)
                        last = (g == ngr - 1 and jj == nt - 1)
                        wv_ = []
                        if jj == 0:
                            wv_ = [tev] + vq_toks
                            if first and prev_oev:
                                wv_.append(prev_oev)
                        tav = P.te(lambda e, gchunk=gchunk, jj=jj, h=h, first=first,
                                   last=last: e.matmul(
                                       OT[:, :],
                                       vq[:, gchunk * 128 + (h % 2) * 64:
                                          gchunk * 128 + (h % 2) * 64 + 64],
                                       pts[:, jj * 128:(jj + 1) * 128],
                                       start=first, stop=last),
                                   wait=wv_,
                                   inc="Ete" if last else None)
                av_done = tav
                wev = [av_done]
                if c == 0 and oh_tok:
                    wev.append(oh_tok)
                toev = P.se(lambda e, c=c: e.copy(oh[:, c * 128:(c + 1) * 128], OT[:, :]),
                            wait=wev, inc="Ese")
                prev_oev = toev
            qk_free = pnorm_tok
            prev_av = av_done
            oh_tok = P.dma("sync", o_sp.ap()[h * 64:(h + 1) * 64, :], oh[:],
                           wait=[prev_oev])
        P.sy(lambda e: e.nop(), wait=[oh_tok])
    P.barrier()


def ph_out_proj(P, wname, bias_name=None, nin=D, idx=None):
    R = P.R
    with P.scope():
        wsb = P.sb("op_w", [128, (nin // 128) * D], BF16)
        wtoks = _load_w_sbuf(P, wsb, wname, nin, D, idx=idx)
        bo = P.sb("op_b", [128, 8], F32) if bias_name else None
        btoks = []
        if bias_name:
            for oc in range(8):
                btoks.append(P.dma("sync", bo[:, oc:oc + 1],
                                   P.dram[bias_name].ap()[oc * 128:(oc + 1) * 128, :]))
        ost = [P.sb(f"op_o{i}", [128, TT], BF16) for i in range(nin // 128)]
        pq = [P.ps(f"op_p{i}", [128, TT]) for i in range(2)]
        o_sp = P.dram["o_sp"]
        pq_tok = [None, None]
        ld_tok = [None] * (nin // 128)
        mm_last = [None] * (nin // 128)
        for tt in range(NTT):
            lds = []
            for dc in range(nin // 128):
                g = [mm_last[dc]] if mm_last[dc] else []
                ld_tok[dc] = P.dma("sync", ost[dc][:],
                                   o_sp.ap()[dc * 128:(dc + 1) * 128,
                                             tt * TT:(tt + 1) * TT], wait=g)
                lds.append(ld_tok[dc])
            for oc in range(8):
                pp = pq[oc % 2]
                gw = [pq_tok[oc % 2]] if pq_tok[oc % 2] else []
                for dc in range(nin // 128):
                    tk = P.te(lambda e, pp=pp, w=_wslice(wsb, D, dc, oc), b=ost[dc],
                              dc=dc, n=nin // 128: e.matmul(
                                  pp[:], w, b[:], start=(dc == 0), stop=(dc == n - 1)),
                              wait=([lds[dc]] + wtoks + btoks + gw) if dc == 0 else [lds[dc]],
                              inc="Ete" if dc == nin // 128 - 1 else None)
                for dc in range(nin // 128):
                    mm_last[dc] = tk
                sl = R[:, oc * CW + 1 + tt * TT: oc * CW + 1 + tt * TT + TT]
                if bias_name:
                    tadd = P.ve(lambda e, pp=pp, sl=sl, oc=oc: e.scalar_tensor_tensor(
                        sl, pp[:], bo[:, oc:oc + 1], sl, ALU.add, ALU.add),
                        wait=[tk], inc="Eve")
                else:
                    tadd = P.ve(lambda e, pp=pp, sl=sl: e.tensor_add(sl, sl, pp[:]),
                                wait=[tk], inc="Eve")
                pq_tok[oc % 2] = tadd
    P.barrier()



def ph_lf_save_x3(P):
    x3 = P.dram["x3_sp"]
    R = P.R
    for c in range(NCH):
        P.dma("sync", x3.ap()[c * 128:(c + 1) * 128, :],
              R[:, c * CW + 1:(c + 1) * CW])
    P.barrier()


def ph_final_x3_ln(P):
    x3 = P.dram["x3_sp"]
    R = P.R
    with P.scope():
        xt = P.sb("fx_t", [128, T], F32)
        for c in range(NCH):
            tok = P.dma("sync", xt[:], x3.ap()[c * 128:(c + 1) * 128, :])
            P.ve(lambda e, c=c: e.tensor_add(
                R[:, c * CW + 1:(c + 1) * CW], R[:, c * CW + 1:(c + 1) * CW],
                xt[:]), wait=[tok])
            tok2 = P.last_tok["vector"]
            P.q("sync", lambda e: e.nop(), wait=[tok2])
    P.barrier()
    emit_ln_inplace(P)


def ph_lf_qkv(P, i):
    R = P.R
    sel_ssq = P.consts["sel_ssq"]
    selb2 = P.consts["sel_bcast2"]
    rq_tm = P.glob["rq_tm"]
    wq = P.dram[f"lf_wqkv_{i}"]
    with P.scope():
        wvsb = P.sb("lq_wv", [128, 8 * 1024], BF16)
        slab = [P.sb(f"lq_s{j}", [128, 8 * 128], BF16) for j in range(2)]
        xst = [P.sb(f"lq_x{j}", [128, TT], BF16) for j in range(8)]
        ev = [P.sb(f"lq_e{j}", [128, TT], BF16) for j in range(4)]
        sqb = P.sb("lq_sq", [128, TT], BF16)
        rr = P.sb("lq_rr", [2, 2 * TT], F32)
        rkbf = P.sb("lq_rk", [2, TT], BF16)
        ksc = P.sb("lq_ks", [128, TT], BF16)
        pq = [P.ps(f"lq_p{j}", [128, TT]) for j in range(2)]
        ssqP = P.ps("lq_ssq", [2, TT])
        auxP = P.ps("lq_aux", [128, TT])
        sem_wv = P.group_sem()
        for dc in range(NCH):
            wvtok = P.dma("gpsimd", wvsb[:, dc * 1024:(dc + 1) * 1024],
                          wq.ap()[dc * 128:(dc + 1) * 128, 2048:3072], sem=sem_wv)
        wvtoks = [wvtok]
        wqr = wq.ap().rearrange("(dc p) n -> p dc n", p=128)
        q_sp, k_sp, v_sp = P.dram["q_sp"], P.dram["k_sp"], P.dram["v_sp"]
        epst24, _ = P.constcol(1e-24)
        nhalf, _ = P.constcol(-0.5)
        last_mm = [None] * 8
        ev_tok = [None] * 4
        pq_tok = [None, None]
        slab_tok = [None, None]
        slab_mm = [None, None]
        aux_free = None
        ssq_free = None
        ksc_dma = None
        evi = 0
        for tt in range(NTT):
            base = 1 + tt * TT
            casts = []
            for c in range(NCH):
                off = 0 if c >= NCH // 2 else 1
                sl = R[:, c * CW + base - 1 + off: c * CW + base - 1 + off + TT]
                g = [last_mm[c]] if last_mm[c] else []
                casts.append(P.ve(lambda e, sl=sl, b=xst[c]: e.tensor_copy(b[:], sl),
                                  wait=g))
            for oc in range(16):
                par = oc % 2
                gm = [slab_mm[par]] if slab_mm[par] else []
                slab_tok[par] = P.dma("gpsimd", slab[par][:],
                                      wqr[:, :, oc * 128:(oc + 1) * 128], wait=gm)
                pp = pq[par]
                gw = [pq_tok[par]] if pq_tok[par] else []
                for dc in range(NCH):
                    tk = P.te(lambda e, pp=pp, par=par, dc=dc: e.matmul(
                        pp[:], slab[par][:, dc * 128:(dc + 1) * 128], xst[dc][:],
                        start=(dc == 0), stop=(dc == NCH - 1)),
                        wait=([casts[dc], slab_tok[par]] + gw) if dc == 0
                        else [casts[dc]],
                        inc="Ete" if dc == NCH - 1 else None)
                slab_mm[par] = tk
                for dc in range(NCH):
                    last_mm[dc] = tk
                eb = ev[evi % 4]
                ge = [ev_tok[evi % 4]] if ev_tok[evi % 4] else []
                tev = P.se(lambda e, pp=pp, b=eb: e.copy(b[:], pp[:]),
                           wait=[tk] + ge, inc="Ese")
                tsq = P.se(lambda e, pp=pp: e.activation(sqb[:], pp[:], AF.Square),
                           wait=[tk])
                pq_tok[par] = tsq
                tss = P.te(lambda e: e.matmul(ssqP[:], sel_ssq[:], sqb[:],
                                              start=True, stop=True),
                           wait=[tsq] + ([ssq_free] if ssq_free else []), inc="Ete")
                ss = rr[:, 0:TT]
                rstd2 = rr[:, TT:2 * TT]
                tc1 = P.ve(lambda e: e.tensor_scalar_max(ss, ssqP[:], 1e-24),
                           wait=[tss])
                ssq_free = tc1
                tc2 = P.se(lambda e: e.activation(rstd2, ss, AF.Ln), wait=[tc1])
                tc3 = P.se(lambda e: e.activation(rstd2, rstd2, AF.Exp,
                                                  scale=nhalf[0:2, :]), wait=[tc2])
                if oc < 8:
                    ev_tok[evi % 4] = P.dma(
                        "sync", q_sp.ap()[oc * 128:(oc + 1) * 128,
                                          tt * TT:(tt + 1) * TT],
                        eb[:], wait=[tev])
                    for tc in range(4):
                        ttr = P.te(lambda e, tc=tc: e.transpose(
                            auxP[0:128, tc * 2: tc * 2 + 2],
                            rstd2[:, tc * 128:(tc + 1) * 128],
                            P.consts["ident_f32"][0:2, 0:2]),
                            wait=[tc3] + ([aux_free] if aux_free and tc == 0
                                          else []),
                            inc="Ete")
                        P.se(lambda e, tc=tc, oc=oc, tt=tt: e.copy(
                            rq_tm[:, (tt * 4 + tc) * 16 + 2 * oc:
                                  (tt * 4 + tc) * 16 + 2 * oc + 2],
                            auxP[0:128, tc * 2: tc * 2 + 2]), wait=[ttr])
                    aux_free = P.last_tok["scalar"]
                else:
                    tc4 = P.ve(lambda e: e.tensor_copy(rkbf[:], rstd2), wait=[tc3])
                    tbc = P.te(lambda e: e.matmul(auxP[:], selb2[:], rkbf[:],
                                                  start=True, stop=True),
                               wait=[tc4] + ([aux_free] if aux_free else []),
                               inc="Ete")
                    tks = P.ve(lambda e, b=eb: e.tensor_mul(ksc[:], b[:], auxP[:]),
                               wait=[tbc, tev]
                               + ([ksc_dma] if ksc_dma else []))
                    aux_free = tks
                    ev_tok[evi % 4] = tks
                    ksc_dma = P.dma(
                        "sync", k_sp.ap()[(oc - 8) * 128:(oc - 7) * 128,
                                          tt * TT:(tt + 1) * TT],
                        ksc[:], wait=[tks])
                evi += 1
            for tc in range(4):
                for half in range(2):
                    par = tc % 2
                    pp = pq[par]
                    gw = [pq_tok[par]] if pq_tok[par] else []
                    for dc in range(NCH):
                        tk = P.te(lambda e, pp=pp, dc=dc, tc=tc, half=half: e.matmul(
                            pp[:], xst[dc][:, tc * 128:(tc + 1) * 128],
                            wvsb[:, dc * 1024 + half * TT: dc * 1024 + half * TT + TT],
                            start=(dc == 0), stop=(dc == NCH - 1)),
                            wait=(gw + wvtoks) if dc == 0 else [],
                            inc="Ete" if dc == NCH - 1 else None)
                    for dc in range(NCH):
                        last_mm[dc] = tk
                    eb = ev[evi % 4]
                    ge = [ev_tok[evi % 4]] if ev_tok[evi % 4] else []
                    tev = P.se(lambda e, pp=pp, b=eb: e.copy(b[:], pp[:]),
                               wait=[tk] + ge, inc="Ese")
                    pq_tok[par] = tev
                    ev_tok[evi % 4] = P.dma(
                        "sync", v_sp.ap()[tt * 4 + tc, :, half * TT:(half + 1) * TT],
                        eb[:], wait=[tev])
                    evi += 1
    P.barrier()


def ph_lf_rope(P, i, which):
    sp = P.dram["q_sp" if which == "q" else "k_sp"]
    ctn = "xp_cq" if which == "q" else "xp_ck"
    stn = "xp_sq" if which == "q" else "xp_sk"
    scn = f"lf_qs_{i}" if which == "q" else f"lf_ks_{i}"
    with P.scope():
        ctab = P.sb("lr_c", [64, T], BF16)
        stab = P.sb("lr_s", [64, T], BF16)
        Atab = P.sb("lr_A", [64, T], BF16)
        Btab = P.sb("lr_B", [64, T], BF16)
        qh = P.sb("lr_qh", [64, T], BF16)
        swld = P.sb("lr_sw", [64, T], BF16)
        T1 = P.sb("lr_T1", [64, T], BF16)
        scol = P.sb("lr_sc", [64, 2], F32)
        t1 = P.dma("sync", ctab[:], P.dram[ctn].ap())
        t2 = P.dma("sync", stab[:], P.dram[stn].ap())
        sc = P.dram[scn].ap().rearrange("(d o) -> d o", o=1)
        t3 = P.dma("sync", scol[:, 0:1], sc)
        t4 = P.dma("sync", scol[32:64, 1:2], sc[0:32])
        t5 = P.dma("sync", scol[0:32, 1:2], sc[32:64])
        ta = P.ve(lambda e: e.tensor_mul(
            Atab[:], ctab[:], scol[:, 0:1].broadcast_to([64, T])),
            wait=[t1, t3])
        tb = P.ve(lambda e: e.tensor_mul(
            Btab[:], stab[:], scol[:, 1:2].broadcast_to([64, T])),
            wait=[t2, t4, t5])
        dq = None
        swfree = None
        for h in range(16):
            g = [dq] if dq else []
            g2 = [swfree] if swfree else []
            l1 = P.dma("sync", qh[:], sp.ap()[h * 64:(h + 1) * 64, :], wait=g)
            l2 = P.dma("sync", swld[0:32, :],
                       sp.ap()[h * 64 + 32: h * 64 + 64, :], wait=g2)
            l3 = P.dma("sync", swld[32:64, :], sp.ap()[h * 64: h * 64 + 32, :],
                       wait=g2)
            a1 = P.ve(lambda e: e.tensor_mul(T1[:], qh[:], Atab[:]),
                      wait=[l1, ta, tb])
            a2 = P.ve(lambda e: e.tensor_mul(qh[:], swld[:], Btab[:]),
                      wait=[l2, l3, a1])
            swfree = a2
            a3 = P.ve(lambda e: e.tensor_add(qh[:], qh[:], T1[:]), wait=[a2])
            dq = P.dma("sync", sp.ap()[h * 64:(h + 1) * 64, :], qh[:], wait=[a3])
    P.barrier()


def ph_lf_attn(P, i):
    rq_tm = P.glob["rq_tm"]
    ident = P.consts["ident"]
    q_sp, k_sp, v_sp, o_sp = (P.dram[n] for n in ("q_sp", "k_sp", "v_sp", "o_sp"))
    with P.scope():
        vq = P.sb("la_vq", [128, 32 * 128], BF16)
        qh = P.sb("la_qh", [64, T], BF16)
        kh = P.sb("la_kh", [64, T], BF16)
        Pb = P.sb("la_P", [128, TT], BF16)
        oh = P.sb("la_oh", [64, T], BF16)
        pts = P.sb("la_pts", [128, TT], BF16)
        rq8 = P.sb("la_rq8", [128, 32 * 16], F32)
        lmask = P.sb("la_lm", [128, 2 * 512], BF16)
        lmask0 = P.sb("la_lm0", [128, 2 * 256], BF16)
        cols = P.sb("la_cols", [128, 4], F32)
        S = P.ps("la_S", [128, TT])
        PT = P.ps("la_PT", [128, TT], BF16)
        OT = P.ps("la_OT", [64, 128])
        mneg = cols[:, 0:1]
        bias = cols[:, 1:2]
        lc = cols[:, 2:3]
        rc_ = cols[:, 3:4]
        tm1 = P.dma("sync", lmask[:], P.dram["locmask"].ap())
        tm2 = P.dma("sync", lmask0[:], P.dram["locmask0"].ap())
        tr8 = P.ve(lambda e: e.tensor_scalar_mul(rq8[:], rq_tm[:], 8.0))
        prev_exp = None
        prev_oev = None
        prev_av = None
        pt_guard = None
        oh_tok = None
        qk_free = None
        vq_toks = []
        for h in range(16):
            if h % 2 == 0:
                gw = [t for t in (prev_av,) if t]
                sem = P.group_sem()
                for j in range(32):
                    vt = P.dma(
                        "sync", vq[:, j * 128:(j + 1) * 128],
                        v_sp.ap()[j, :, (h // 2) * 128:(h // 2 + 1) * 128],
                        wait=gw if j == 0 else [], sem=sem)
                vq_toks = [vt]
            gw = [qk_free] if qk_free else []
            tq = P.dma("sync", qh[:], q_sp.ap()[h * 64:(h + 1) * 64, :], wait=gw)
            tk_ = P.dma("sync", kh[:], k_sp.ap()[h * 64:(h + 1) * 64, :], wait=gw)
            for w in range(16):
                for p_ in range(2):
                    tc = 2 * w + p_
                    if w == 0:
                        kstart, kw = 0, 256
                        mk = lmask0[:, p_ * 256:(p_ + 1) * 256]
                    else:
                        kstart, kw = w * 256 - 256, 512
                        mk = lmask[:, p_ * 512:(p_ + 1) * 512]
                    t_qk = P.te(lambda e, tc=tc, kstart=kstart, kw=kw: e.matmul(
                        S[:, 0:kw], qh[:, tc * 128:(tc + 1) * 128],
                        kh[:, kstart: kstart + kw], start=True, stop=False),
                        wait=[tq, tk_] + ([prev_exp] if prev_exp else []))
                    t_qk = P.te(lambda e, mk=mk, kw=kw: e.matmul(
                        S[:, 0:kw], ident[:], mk, start=False, stop=True),
                        wait=[tm1, tm2], inc="Ete")
                    tm = P.ve(lambda e, kw=kw: e.reduce_max(
                        mneg, S[:, 0:kw], axis=AX.X, negate=True),
                        wait=[t_qk, tr8])
                    sccol = rq8[:, tc * 16 + h: tc * 16 + h + 1]
                    tb = P.ve(lambda e, sccol=sccol: e.tensor_mul(bias, mneg, sccol),
                              wait=[tm])
                    texp = P.se(lambda e, kw=kw, sccol=sccol: e.activation(
                        Pb[:, 0:kw], S[:, 0:kw], AF.Exp,
                        bias=bias, scale=sccol, accum_out=lc), wait=[tb])
                    prev_exp = texp
                    tr0 = P.ve(lambda e: e.reciprocal(rc_, lc), wait=[texp])
                    tnorm = P.ve(lambda e, kw=kw: e.tensor_scalar_mul(
                        Pb[:, 0:kw], Pb[:, 0:kw], rc_), wait=[tr0])
                    nt = kw // 128
                    for jj in range(nt):
                        tpt = P.te(lambda e, jj=jj: e.transpose(
                            PT[:, jj * 128:(jj + 1) * 128],
                            Pb[:, jj * 128:(jj + 1) * 128], ident[:]),
                            wait=([tnorm] + ([pt_guard] if pt_guard else []))
                            if jj == 0 else [],
                            inc="Ete" if jj == nt - 1 else None)
                    tev = P.se(lambda e, kw=kw: e.copy(pts[:, 0:kw], PT[:, 0:kw]),
                               wait=[tpt], inc="Ese")
                    pt_guard = tev
                    kc0 = kstart // 128
                    for jj in range(nt):
                        wv_ = []
                        if jj == 0:
                            wv_ = [tev] + vq_toks
                            if prev_oev:
                                wv_.append(prev_oev)
                        tav = P.te(lambda e, jj=jj, kc0=kc0, h=h, nt=nt: e.matmul(
                            OT[:, :],
                            vq[:, (kc0 + jj) * 128 + (h % 2) * 64:
                               (kc0 + jj) * 128 + (h % 2) * 64 + 64],
                            pts[:, jj * 128:(jj + 1) * 128],
                            start=(jj == 0), stop=(jj == nt - 1)),
                            wait=wv_, inc="Ete" if jj == nt - 1 else None)
                    prev_av = tav
                    wev = [tav]
                    if tc == 0 and oh_tok:
                        wev.append(oh_tok)
                    toev = P.se(lambda e, tc=tc: e.copy(
                        oh[:, tc * 128:(tc + 1) * 128], OT[:, :]), wait=wev,
                        inc="Ese")
                    prev_oev = toev
                    qk_free = tnorm
            oh_tok = P.dma("sync", o_sp.ap()[h * 64:(h + 1) * 64, :], oh[:],
                           wait=[prev_oev])
        P.sy(lambda e: e.nop(), wait=[oh_tok])
    P.barrier()


def ph_lf_geglu(P, i):
    R = P.R
    ones_bf = P.consts["ones_col_bf"]
    ones_row = P.consts["ones_row_f32"]
    w1 = P.dram[f"lf_w1_{i}"]
    w2 = P.dram[f"lf_w2_{i}"]
    NOC = 44
    NHC = 22
    with P.scope():
        xst = [P.sb(f"lg_x{j}", [128, TT], BF16) for j in range(2)]
        sqst = [P.sb(f"lg_q{j}", [128, TT], BF16) for j in range(2)]
        xnst = [P.sb(f"lg_n{j}", [128, TT], BF16) for j in range(8)]
        hbuf = P.sb("lg_h", [128, NHC * TT], BF16)
        abuf = P.sb("lg_a", [128, TT], BF16)
        gbuf = P.sb("lg_g", [128, TT], BF16)
        tmpf = P.sb("lg_t", [128, TT], F32)
        rowbuf = P.sb("lg_rows", [1, 6 * TT], F32)
        bias1 = P.sb("lg_b1", [128, NOC], F32)
        gcol = P.sb("lg_gc", [128, 8], F32)
        bcol = P.sb("lg_bc", [128, 8], F32)
        b2col = P.sb("lg_b2", [128, 8], F32)
        slab = [P.sb(f"lg_s{j}", [128, 8 * 128], BF16) for j in range(2)]
        slab2 = [P.sb(f"lg_s2{j}", [128, 1024], BF16) for j in range(2)]
        pool = P.ps("lg_pool", [128, 8 * TT])
        sP = pool[0:1, 4 * TT: 5 * TT]
        qP = pool[0:1, 5 * TT: 6 * TT]
        rbP = pool[:, 6 * TT: 7 * TT]
        mbP = pool[:, 7 * TT: 8 * TT]
        mean = rowbuf[:, 0:TT]
        ex2 = rowbuf[:, TT:2 * TT]
        var = rowbuf[:, 2 * TT:3 * TT]
        sd = rowbuf[:, 3 * TT:4 * TT]
        rstd = rowbuf[:, 4 * TT:5 * TT]
        mr = rowbuf[:, 5 * TT:6 * TT]
        bt = [P.dma("sync", bias1[:], P.dram[f"lf_b1c_{i}"].ap()),
              P.dma("sync", gcol[:], P.dram[f"lf_gc_{i}"].ap()),
              P.dma("sync", bcol[:], P.dram[f"lf_bc_{i}"].ap()),
              P.dma("sync", b2col[:], P.dram[f"lf_b2c_{i}"].ap())]
        w1r = w1.ap().rearrange("(dc p) n -> p dc n", p=128)
        epst, _ = P.constcol(EPS_LN)
        nhalf, _ = P.constcol(-0.5)
        stat_guard = None
        pool_y_free = None
        slab_tok = [None, None]
        slab_mm = [None, None]
        slab2_tok = [None, None]
        slab2_mm = [None, None]
        xn_guard = [None] * 8
        for tt in range(NTT):
            base = 1 + tt * TT
            stat_toks = [None, None]
            for c in range(NCH):
                off = 0 if c >= NCH // 2 else 1
                sl = R[:, c * CW + base - 1 + off: c * CW + base - 1 + off + TT]
                par = c % 2
                gpar = [stat_toks[par]] if stat_toks[par] else []
                tcast = P.ve(lambda e, sl=sl, b=xst[par]: e.tensor_copy(b[:], sl),
                             wait=gpar + ([pool_y_free] if pool_y_free and c == 0
                                          else []))
                tsq = P.se(lambda e, sl=sl, b=sqst[par]: e.activation(
                    b[:], sl, AF.Square),
                    wait=gpar + ([pool_y_free] if pool_y_free and c == 0 else []))
                tmm1 = P.te(lambda e, b=xst[par], c=c: e.matmul(
                    sP, ones_bf[:], b[:], start=(c == 0), stop=(c == NCH - 1)),
                    wait=[tcast] + ([pool_y_free] if pool_y_free and c == 0 else []))
                tmm2 = P.te(lambda e, b=sqst[par], c=c: e.matmul(
                    qP, ones_bf[:], b[:], start=(c == 0), stop=(c == NCH - 1)),
                    wait=[tsq], inc="Ete")
                stat_toks[par] = tmm2
            mm_done = tmm2
            t1 = P.ve(lambda e: e.tensor_scalar_mul(mean, sP, 1.0 / D),
                      wait=[mm_done])
            t2 = P.ve(lambda e: e.tensor_scalar_mul(ex2, qP, 1.0 / D))
            t3 = P.ve(lambda e: e.tensor_mul(var, mean, mean), wait=[t1, t2])
            tv = P.ve(lambda e: e.tensor_sub(var, ex2, var), wait=[t3])
            tln = P.se(lambda e: e.activation(sd, var, AF.Ln, bias=epst[0:1, :]),
                       wait=[tv])
            ts = P.se(lambda e: e.activation(rstd, sd, AF.Exp,
                                             scale=nhalf[0:1, :]), wait=[tln])
            tmr = P.ve(lambda e: e.tensor_mul(mr, mean, rstd), wait=[ts])
            tb1 = P.te(lambda e: e.matmul(rbP, ones_row[:], rstd, start=True,
                                          stop=True), wait=[tmr], inc="Ete")
            tb2 = P.te(lambda e: e.matmul(mbP, ones_row[:], mr, start=True,
                                          stop=True), inc="Ete")
            xn_done = []
            tn3 = None
            for c in range(NCH):
                off = 0 if c >= NCH // 2 else 1
                sl = R[:, c * CW + base - 1 + off: c * CW + base - 1 + off + TT]
                g = [xn_guard[c]] if xn_guard[c] else []
                tn1 = P.ve(lambda e, sl=sl: e.tensor_mul(tmpf[:], sl, rbP),
                           wait=[tb1, tb2] + ([tn3] if tn3 else []))
                tn2 = P.ve(lambda e: e.tensor_sub(tmpf[:], tmpf[:], mbP))
                tn3 = P.se(lambda e, c=c: e.activation(
                    xnst[c][:], tmpf[:], AF.Identity, scale=gcol[:, c:c + 1],
                    bias=bcol[:, c:c + 1]), wait=[tn2] + bt + g)
                xn_done.append(tn3)
            stat_guard = tn3
            tmul = hbuf_guard if tt > 0 else None
            hp_guard = [None, None]
            for oc in range(NOC):
                par = oc % 2
                gm = [slab_mm[par]] if slab_mm[par] else []
                slab_tok[par] = P.dma("gpsimd", slab[par][:],
                                      w1r[:, :, oc * 128:(oc + 1) * 128], wait=gm)
                Hp = pool[:, par * TT:(par + 1) * TT]
                bankw = [slab_tok[par]]
                if oc < 2 and pool_y_free:
                    bankw.append(pool_y_free)
                if hp_guard[par]:
                    bankw.append(hp_guard[par])
                for dc in range(NCH):
                    tk = P.te(lambda e, Hp=Hp, par=par, dc=dc: e.matmul(
                        Hp, slab[par][:, dc * 128:(dc + 1) * 128], xnst[dc][:],
                        start=(dc == 0), stop=(dc == NCH - 1)),
                        wait=(bankw + [xn_done[dc]]) if dc == 0 else [xn_done[dc]],
                        inc="Ete" if dc == NCH - 1 else None)
                slab_mm[par] = tk
                if oc % 2 == 0:
                    ta_ev = P.se(lambda e, Hp=Hp, oc=oc: e.activation(
                        abuf[:], Hp, AF.Identity, bias=bias1[:, oc:oc + 1]),
                        wait=[tk] + ([tmul] if tmul else []), inc="Ese")
                    hp_guard[par] = ta_ev
                else:
                    tg_ev = P.se(lambda e, Hp=Hp, oc=oc: e.activation(
                        gbuf[:], Hp, AF.Gelu, bias=bias1[:, oc:oc + 1]),
                        wait=[tk], inc="Ese")
                    hp_guard[par] = tg_ev
                    tmul = P.ve(lambda e, hcx=oc // 2: e.tensor_mul(
                        hbuf[:, hcx * TT:(hcx + 1) * TT], abuf[:], gbuf[:]),
                        wait=[ta_ev, tg_ev])
            h_done = tmul
            wlast = None
            for hc in range(NHC):
                par = hc % 2
                gm = [slab2_mm[par]] if slab2_mm[par] else []
                slab2_tok[par] = P.dma("gpsimd", slab2[par][:],
                                       w2.ap()[hc * 128:(hc + 1) * 128, :],
                                       wait=gm)
                for dc in range(NCH):
                    tk = P.te(lambda e, par=par, dc=dc, hc=hc: e.matmul(
                        pool[:, dc * TT:(dc + 1) * TT],
                        slab2[par][:, dc * 128:(dc + 1) * 128],
                        hbuf[:, hc * TT:(hc + 1) * TT],
                        start=(hc == 0), stop=(hc == NHC - 1)),
                        wait=[slab2_tok[par], h_done] if dc == 0
                        else [slab2_tok[par]],
                        inc="Ete" if dc == NCH - 1 else None)
                slab2_mm[par] = tk
                wlast = tk
            radds = []
            for c in range(NCH):
                sl = R[:, c * CW + base: c * CW + base + TT]
                radds.append(P.ve(lambda e, sl=sl, c=c: e.scalar_tensor_tensor(
                    sl, pool[:, c * TT:(c + 1) * TT], b2col[:, c:c + 1], sl,
                    ALU.add, ALU.add), wait=[wlast] if c == 0 else []))
            pool_y_free = radds[-1]
            hbuf_guard = wlast
            lastw1 = slab_mm[(NOC - 1) % 2]
            for c in range(NCH):
                xn_guard[c] = lastw1
    P.barrier()


def ph_mla_proj(P):
    R = P.R
    with P.scope():
        wqsb = P.sb("ml_wq", [128, 8 * D], BF16)
        wdsb = P.sb("ml_wd", [128, 8 * 128], BF16)
        wusb = P.sb("ml_wu", [128, 2048], BF16)
        latsb = P.sb("ml_lat", [128, T], BF16)
        xst = [P.sb(f"ml_x{i}", [128, TT], BF16) for i in range(8)]
        ev = [P.sb(f"ml_e{i}", [128, TT], BF16) for i in range(4)]
        pq = [P.ps(f"ml_p{i}", [128, TT]) for i in range(3)]
        wtoks = (_load_w_sbuf(P, wqsb, "mla_wq", D, D)
                 + _load_w_sbuf(P, wdsb, "mla_down", D, 128)
                 + _load_w_sbuf(P, wusb, "mla_up", 128, 2048))
        q_sp, k_sp, v_sp = P.dram["q_sp"], P.dram["k_sp"], P.dram["v_sp"]
        last_mm = [None] * 8
        ev_tok = [None] * 4
        pq_tok = [None] * 3
        pqi = evi = 0

        def step(mmfn, nacc, out_dst, extra_wait=()):
            nonlocal pqi, evi
            pp = pq[pqi % 3]
            gw = [pq_tok[pqi % 3]] if pq_tok[pqi % 3] else []
            for i in range(nacc):
                tk = P.te(lambda e, i=i, pp=pp: mmfn(e, pp, i),
                          wait=(list(extra_wait) + wtoks + gw) if i == 0 else [],
                          inc="Ete" if i == nacc - 1 else None)
            eb = ev[evi % 4]
            ge = [ev_tok[evi % 4]] if ev_tok[evi % 4] else []
            tev = P.se(lambda e, pp=pp, b=eb: e.copy(b[:], pp[:]),
                       wait=[tk] + ge, inc="Ese")
            pq_tok[pqi % 3] = tev
            if out_dst is not None:
                ev_tok[evi % 4] = P.dma("sync", out_dst, eb[:], wait=[tev])
            pqi += 1
            evi += 1
            return tk, tev, eb

        for tt in range(NTT):
            base = 1 + tt * TT
            casts = []
            for c in range(NCH):
                sl = R[:, c * CW + base: c * CW + base + TT]
                g = [last_mm[c]] if last_mm[c] else []
                casts.append(P.ve(lambda e, sl=sl, b=xst[c]: e.tensor_copy(b[:], sl),
                                  wait=g))
            pp = pq[pqi % 3]
            gw = [pq_tok[pqi % 3]] if pq_tok[pqi % 3] else []
            for dc in range(NCH):
                tk = P.te(lambda e, dc=dc, pp=pp: e.matmul(
                    pp[:], _wslice(wdsb, 128, dc, 0), xst[dc][:],
                    start=(dc == 0), stop=(dc == NCH - 1)),
                    wait=([casts[dc]] + wtoks + gw) if dc == 0 else [casts[dc]],
                    inc="Ete" if dc == NCH - 1 else None)
            lat_ev = P.se(lambda e, pp=pp, tt=tt: e.copy(
                latsb[:, tt * TT:(tt + 1) * TT], pp[:]), wait=[tk], inc="Ese")
            pq_tok[pqi % 3] = lat_ev
            pqi += 1
            for dc in range(NCH):
                last_mm[dc] = tk
            for oc in range(8):
                def qmm(e, pp, i, oc=oc):
                    return e.matmul(pp[:], _wslice(wqsb, D, i, oc), xst[i][:],
                                    start=(i == 0), stop=(i == NCH - 1))
                tk, _, _ = step(qmm, NCH,
                                q_sp.ap()[oc * 128:(oc + 1) * 128, tt * TT:(tt + 1) * TT],
                                extra_wait=casts)
                for dc in range(NCH):
                    last_mm[dc] = tk
            for oc in range(8):
                def kmm(e, pp, i, oc=oc, tt=tt):
                    return e.matmul(pp[:], wusb[:, oc * 128:(oc + 1) * 128],
                                    latsb[:, tt * TT:(tt + 1) * TT],
                                    start=True, stop=True)
                step(kmm, 1,
                     k_sp.ap()[oc * 128:(oc + 1) * 128, tt * TT:(tt + 1) * TT],
                     extra_wait=[lat_ev])
            for tc in range(4):
                for half in range(2):
                    def vmm(e, pp, i, tc=tc, half=half, tt=tt):
                        return e.matmul(
                            pp[:], latsb[:, tt * TT + tc * 128: tt * TT + (tc + 1) * 128],
                            wusb[:, 1024 + half * TT: 1024 + (half + 1) * TT],
                            start=True, stop=True)
                    step(vmm, 1,
                         v_sp.ap()[tt * 4 + tc, :, half * TT:(half + 1) * TT],
                         extra_wait=[lat_ev])
    P.barrier()



def ph_moe_precast(P):
    w1 = P.dram["moe_w1"]
    w2 = P.dram["moe_w2"]
    w1bf = P.dram["w1bf"]
    w2bf = P.dram["w2bf"]
    with P.scope():
        slab = [P.sb(f"mc_s{i}", [128, 1024], BF16) for i in range(4)]
        st_tok = [None] * 4
        i = 0
        for e in range(8):
            w1r = w1.ap()[e].rearrange("(dc p) j -> p dc j", p=128)
            for hc in range(32):
                b = slab[i % 4]
                g = [st_tok[i % 4]] if st_tok[i % 4] else []
                t1 = P.dma("gpsimd", b[:],
                           w1r[:, :, hc * 128:(hc + 1) * 128], wait=g)
                st_tok[i % 4] = P.dma("sync", w1bf.ap()[e, hc], b[:], wait=[t1])
                i += 1
                b = slab[i % 4]
                g = [st_tok[i % 4]] if st_tok[i % 4] else []
                t2 = P.dma("gpsimd", b[:],
                           w2.ap()[e][hc * 128:(hc + 1) * 128, :], wait=g)
                st_tok[i % 4] = P.dma("sync", w2bf.ap()[e, hc], b[:], wait=[t2])
                i += 1
    P.barrier()


def ph_moe(P):
    R = P.R
    ident = P.consts["ident"]
    with P.scope():
        xst = [P.sb(f"mo_x{i}", [128, TT], BF16) for i in range(8)]
        Hsb = P.sb("mo_H", [128, 32 * TT], BF16)
        Ysb = P.sb("mo_Y", [128, 8 * TT], F32)
        w1s = [P.sb(f"mo_w1{i}", [128, 1024], BF16) for i in range(2)]
        w2s = [P.sb(f"mo_w2{i}", [128, 1024], BF16) for i in range(2)]
        cwfm = P.sb("mo_cwfm", [8, TT], BF16)
        cwb_sb = P.sb("mo_cwb", [128, TT], BF16)
        b1col = P.sb("mo_b1c", [128, 32], F32)
        b2sb = P.sb("mo_b2", [8, D], BF16)
        wgsb = P.sb("mo_wg", [128, 8 * 8], BF16)
        tmpf = P.sb("mo_tmp", [128, TT], F32)
        e8 = P.sb("mo_e8", [128, 8], F32)
        k8 = P.sb("mo_k8", [128, 8], F32)
        cw8 = P.sb("mo_cw8", [128, 8], F32)
        cwb8 = P.sb("mo_cwb8", [128, 8], BF16)
        colm = P.sb("mo_colm", [128, 4], F32)
        pool = P.ps("mo_pool", [128, 8 * TT])
        ptr = pool[0:8, 6 * TT + 64: 6 * TT + 128].bitcast(BF16)
        sel8 = P.consts["sel_bcast8"]

        m1neg = colm[:, 0:1]
        m2c = colm[:, 1:2]
        lc = colm[:, 2:3]
        rcp = colm[:, 3:4]

        wg_tok = P.dma("gpsimd", wgsb[:],
                       P.dram["moe_wg"].ap().rearrange("(dc p) j -> p dc j", p=128))
        b2_tok = P.dma("gpsimd", b2sb[:], P.dram["moe_b2"].ap())
        w1bf, w2bf, b1d = P.dram["w1bf"], P.dram["w2bf"], P.dram["moe_b1"]

        cast_guard = None
        yread_guard = None
        slab_tok = [None, None]
        slab2_tok = [None, None]
        slab_mm = [None, None]
        slab2_mm = [None, None]
        pool_free = None
        hsb_guard = None
        cwfm_guard = None
        for tt in range(NTT):
            base = 1 + tt * TT
            casts = []
            for c in range(NCH):
                sl = R[:, c * CW + base: c * CW + base + TT]
                g = [cast_guard] if cast_guard else []
                casts.append(P.ve(lambda e_, sl=sl, b=xst[c]: e_.tensor_copy(b[:], sl),
                                  wait=g))
            Gp = pool[:, 6 * TT: 6 * TT + 32]
            for tc in range(4):
                for dc in range(NCH):
                    tkg = P.te(lambda e_, tc=tc, dc=dc: e_.matmul(
                        Gp[:, tc * 8:(tc + 1) * 8],
                        xst[dc][:, tc * 128:(tc + 1) * 128],
                        wgsb[:, dc * 8:(dc + 1) * 8],
                        start=(dc == 0), stop=(dc == NCH - 1)),
                        wait=[casts[dc], wg_tok, b2_tok]
                        + ([pool_free] if pool_free and dc == 0 else []),
                        inc="Ete" if dc == NCH - 1 else None)
                G = Gp[:, tc * 8:(tc + 1) * 8]
                tm = P.ve(lambda e_, G=G: e_.reduce_max(m1neg, G, axis=AX.X,
                                                        negate=True), wait=[tkg])
                tex = P.se(lambda e_, G=G: e_.activation(
                    e8[:], G, AF.Exp, bias=m1neg, accum_out=lc), wait=[tm])
                teq = P.ve(lambda e_: e_.tensor_single_scalar(
                    k8[:], e8[:], 1.0, ALU.is_ge), wait=[tex])
                tpm = P.ve(lambda e_: e_.scalar_tensor_tensor(
                    cw8[:], k8[:], -1e30, e8[:], ALU.mult, ALU.add), wait=[teq])
                tm2 = P.ve(lambda e_: e_.reduce_max(m2c, cw8[:], axis=AX.X),
                           wait=[tpm])
                tke = P.ve(lambda e_: e_.tensor_single_scalar(
                    k8[:], e8[:], m2c, ALU.is_ge), wait=[tm2])
                trr = P.ve(lambda e_: e_.reciprocal(rcp, lc), wait=[tex])
                tcw = P.ve(lambda e_: e_.tensor_mul(cw8[:], e8[:], k8[:]),
                           wait=[tke, trr])
                tcw2 = P.ve(lambda e_: e_.tensor_scalar_mul(cw8[:], cw8[:], rcp),
                            wait=[tcw])
                tcb = P.ve(lambda e_: e_.tensor_copy(cwb8[:], cw8[:]), wait=[tcw2])
                prevcw = [cw_done] if tc > 0 else (
                    [cwfm_guard] if cwfm_guard else [])
                ttr = P.te(lambda e_: e_.transpose(ptr, cwb8[:], ident[:]),
                           wait=[tcb] + prevcw, inc="Ete")
                tev = P.se(lambda e_, tc=tc: e_.copy(
                    cwfm[:, tc * 128:(tc + 1) * 128], ptr), wait=[ttr])
                cw_done = tev
            yg = [yread_guard] if yread_guard else []
            for dc in range(NCH):
                tb2 = P.te(lambda e_, dc=dc: e_.matmul(
                    pool[:, dc * TT:(dc + 1) * TT],
                    b2sb[:, dc * 128:(dc + 1) * 128], cwfm[:],
                    start=True, stop=True), wait=[cw_done], inc="Ete")
                tev = P.se(lambda e_, dc=dc: e_.copy(
                    Ysb[:, dc * TT:(dc + 1) * TT], pool[:, dc * TT:(dc + 1) * TT]),
                    wait=[tb2] + yg)
                yinit_done = tev
            b1_guard = None
            for e in range(8):
                b1tok = P.dma("sync", b1col[:], b1d.ap()[e],
                              wait=[b1_guard] if b1_guard else [])
                tcwb = P.te(lambda e_, e=e: e_.matmul(
                    pool[:, 6 * TT: 7 * TT], sel8[:, e * 128:(e + 1) * 128],
                    cwfm[:], start=True, stop=True),
                    wait=[cw_done] + ([pool_free] if pool_free else [])
                    + ([yinit_done] if e == 0 else []), inc="Ete")
                tcwe = P.se(lambda e_: e_.copy(cwb_sb[:], pool[:, 6 * TT: 7 * TT]),
                            wait=[tcwb])
                gelu_last = None
                hp_guard = [None, None]
                for hc in range(32):
                    par = hc % 2
                    gm = [slab_mm[par]] if slab_mm[par] else []
                    slab_tok[par] = P.dma(
                        "sync", w1s[par][:], w1bf.ap()[e, hc], wait=gm)
                    Hp = pool[:, par * TT:(par + 1) * TT]
                    bankw = []
                    if hc < 2:
                        if pool_free:
                            bankw.append(pool_free)
                        if e == 0:
                            bankw.append(yinit_done)
                    if hp_guard[par]:
                        bankw.append(hp_guard[par])
                    for dc in range(NCH):
                        tkm = P.te(lambda e_, par=par, dc=dc, Hp=Hp: e_.matmul(
                            Hp, w1s[par][:, dc * 128:(dc + 1) * 128], xst[dc][:],
                            start=(dc == 0), stop=(dc == NCH - 1)),
                            wait=([slab_tok[par]] + bankw) if dc == 0
                            else [slab_tok[par]],
                            inc="Ete" if dc == NCH - 1 else None)
                    slab_mm[par] = tkm
                    gelu_last = P.se(lambda e_, hc=hc, Hp=Hp: e_.activation(
                        Hsb[:, hc * TT:(hc + 1) * TT], Hp, AF.Gelu,
                        bias=b1col[:, hc:hc + 1]), wait=[tkm, b1tok])
                    hp_guard[par] = gelu_last
                b1_guard = gelu_last
                wlast = None
                for hc in range(32):
                    par = hc % 2
                    g = [slab2_tok[par]] if slab2_tok[par] else []
                    gm = [slab2_mm[par]] if slab2_mm[par] else []
                    slab2_tok[par] = P.dma(
                        "sync", w2s[par][:], w2bf.ap()[e, hc], wait=gm)
                    for dc in range(NCH):
                        tkm = P.te(lambda e_, par=par, dc=dc, hc=hc: e_.matmul(
                            pool[:, dc * TT:(dc + 1) * TT],
                            w2s[par][:, dc * 128:(dc + 1) * 128],
                            Hsb[:, hc * TT:(hc + 1) * TT],
                            start=(hc == 0), stop=(hc == 31)),
                            wait=[slab2_tok[par], gelu_last] if dc == 0
                            else [slab2_tok[par]],
                            inc="Ete" if dc == NCH - 1 else None)
                    slab2_mm[par] = tkm
                    wlast = tkm
                hsb_guard = wlast
                for dc in range(NCH):
                    t1 = P.ve(lambda e_, dc=dc: e_.tensor_mul(
                        tmpf[:], pool[:, dc * TT:(dc + 1) * TT], cwb_sb[:]),
                        wait=[wlast, tcwe] if dc == 0 else [])
                    t2 = P.ve(lambda e_, dc=dc: e_.tensor_add(
                        Ysb[:, dc * TT:(dc + 1) * TT],
                        Ysb[:, dc * TT:(dc + 1) * TT], tmpf[:]),
                        wait=[t1] + ([yinit_done] if dc == 0 else []))
                pool_free = t2
            radds = []
            for c in range(NCH):
                sl = R[:, c * CW + base: c * CW + base + TT]
                radds.append(P.ve(lambda e_, sl=sl, c=c: e_.tensor_add(
                    sl, sl, Ysb[:, c * TT:(c + 1) * TT]), wait=[pool_free]))
            yread_guard = radds[-1]
            cast_guard = wlast
            cwfm_guard = tcwb
    P.barrier()



WEIGHT_SPECS = {
    "l0_wq": (D, D), "l0_wk": (D, D), "l0_wv": (D, D), "l0_wo": (D, D),
    "l0_bo": (D, 1),
    "moe_wg": (D, 8), "moe_w1": (8, D, 4096), "moe_b1": (8, 128, 32),
    "moe_w2": (8, 4096, D), "moe_b2": (8, D),
    "mla_down": (D, 128), "mla_up": (128, 2048), "mla_wq": (D, D), "mla_wo": (D, D),
}
for _i in range(6):
    WEIGHT_SPECS[f"lf_wqkv_{_i}"] = (D, 3072)
    WEIGHT_SPECS[f"lf_qs_{_i}"] = (64,)
    WEIGHT_SPECS[f"lf_ks_{_i}"] = (64,)
    WEIGHT_SPECS[f"lf_wo_{_i}"] = (D, D)
    WEIGHT_SPECS[f"lf_w1_{_i}"] = (D, 5632)
    WEIGHT_SPECS[f"lf_b1c_{_i}"] = (128, 44)
    WEIGHT_SPECS[f"lf_w2_{_i}"] = (2816, D)
    WEIGHT_SPECS[f"lf_gc_{_i}"] = (128, 8)
    WEIGHT_SPECS[f"lf_bc_{_i}"] = (128, 8)
    WEIGHT_SPECS[f"lf_b2c_{_i}"] = (128, 8)



class LazyDram(dict):

    def __init__(self, nc):
        super().__init__()
        self.nc = nc
        self.used = []

    def __missing__(self, name):
        if name not in WEIGHT_SPECS:
            raise KeyError(name)
        h = self.nc.dram_tensor(name, list(WEIGHT_SPECS[name]), F32,
                                kind="ExternalInput")
        self[name] = h
        self.used.append(name)
        return h


def build_program(phases, extra_inputs=(), extra_outputs=()):
    stack = contextlib.ExitStack()
    nc = bass.Bass("TRN2", target_bir_lowering=False)
    P = Prog(nc, stack)

    P.dram = LazyDram(nc)
    P.dram["x_fm"] = nc.dram_tensor("x_fm", [D, T], F32, kind="ExternalInput")
    P.dram["out"] = nc.dram_tensor("out", [D, T], F32, kind="ExternalOutput")
    P.dram["q_sp"] = nc.dram_tensor("q_sp", [D, T], BF16, kind="Internal")
    P.dram["k_sp"] = nc.dram_tensor("k_sp", [D, T], BF16, kind="Internal")
    P.dram["v_sp"] = nc.dram_tensor("v_sp", [32, 128, D], BF16, kind="Internal")
    P.dram["o_sp"] = nc.dram_tensor("o_sp", [D, T], BF16, kind="Internal")
    P.dram["x3_sp"] = nc.dram_tensor("x3_sp", [D, T], F32, kind="Internal")
    P.dram["w1bf"] = nc.dram_tensor("w1bf", [8, 32, 128, 1024], BF16, kind="Internal")
    P.dram["w2bf"] = nc.dram_tensor("w2bf", [8, 32, 128, 1024], BF16, kind="Internal")
    in_names = ["x_fm"]
    for name, shape in extra_inputs:
        P.dram[name] = nc.dram_tensor(name, list(shape), F32, kind="ExternalInput")
        in_names.append(name)
    for name, shape in extra_outputs:
        P.dram[name] = nc.dram_tensor(name, list(shape), F32, kind="ExternalOutput")

    with stack:
        P.R = P.sb("R", [128, NCH * CW], F32)
        P.glob = {"rq_tm": P.sb("rq_tm", [128, 32 * 16], F32)}
        ph_const_load(P)
        for emitter, kw in phases:
            emitter(P, **kw)
        in_names.extend(P.dram.used)
        with (nc.Block() as blk):
            @blk.sync
            def _(e):
                P.emit("sync", e)

            @blk.tensor
            def _(e):
                P.emit("tensor", e)

            @blk.vector
            def _(e):
                P.emit("vector", e)

            @blk.scalar
            def _(e):
                P.emit("scalar", e)

            @blk.gpsimd
            def _(e):
                P.emit("gpsimd", e)
    return nc, in_names


def ph_add_extra(P, src="x3in"):
    xd = P.dram[src]
    R = P.R
    with P.scope():
        xt = P.sb("ax_t", [128, T], F32)
        for c in range(NCH):
            tok = P.dma("sync", xt[:], xd.ap()[c * 128:(c + 1) * 128, :])
            tadd = P.ve(lambda e, c=c: e.tensor_add(
                R[:, c * CW + 1:(c + 1) * CW], R[:, c * CW + 1:(c + 1) * CW],
                xt[:]), wait=[tok])
            P.q("sync", lambda e: e.nop(), wait=[tadd])
    P.barrier()


def _block_phases():
    g1 = [
        (ph_l0_proj, {}),
        (emit_causal_attention, {"rope": True, "scale": 0.125}),
        (ph_out_proj, {"wname": "l0_wo", "bias_name": "l0_bo"}),
        (emit_ln_inplace, {}),
    ]
    g2 = [
        (ph_moe_precast, {}),
        (ph_moe, {}),
        (emit_ln_inplace, {}),
    ]
    g3 = [
        (ph_mla_proj, {}),
        (emit_causal_attention, {"rope": False, "scale": 0.125}),
        (ph_out_proj, {"wname": "mla_wo"}),
        (emit_ln_inplace, {}),
    ]
    g4 = [(ph_lf_save_x3, {})]
    for i in range(6):
        g4 += [
            (ph_lf_qkv, {"i": i}),
            (ph_lf_rope, {"i": i, "which": "q"}),
            (ph_lf_rope, {"i": i, "which": "k"}),
            (ph_lf_attn, {"i": i}),
            (ph_out_proj, {"wname": f"lf_wo_{i}"}),
            (ph_lf_geglu, {"i": i}),
        ]
    g4 += [(ph_final_x3_ln, {})]
    return [g1, g2, g3, g4]


FUSED = False


def _launch_groups():
    gs = []
    gs.append(([
        (ph_l0_proj, {}),
        (emit_causal_attention, {"rope": True, "scale": 0.125}),
        (ph_out_proj, {"wname": "l0_wo", "bias_name": "l0_bo"}),
        (emit_ln_inplace, {}),
    ], ()))
    gs.append(([
        (ph_moe_precast, {}),
        (ph_moe, {}),
        (emit_ln_inplace, {}),
    ], ()))
    gs.append(([
        (ph_mla_proj, {}),
        (emit_causal_attention, {"rope": False, "scale": 0.125}),
        (ph_out_proj, {"wname": "mla_wo"}),
        (emit_ln_inplace, {}),
    ], ()))
    def lf_layer(i):
        return [
            (ph_lf_qkv, {"i": i}),
            (ph_lf_rope, {"i": i, "which": "q"}),
            (ph_lf_rope, {"i": i, "which": "k"}),
            (ph_lf_attn, {"i": i}),
            (ph_out_proj, {"wname": f"lf_wo_{i}"}),
            (ph_lf_geglu, {"i": i}),
        ]
    gs.append((lf_layer(0) + lf_layer(1), ()))
    gs.append((lf_layer(2) + lf_layer(3), ()))
    gs.append((lf_layer(4) + lf_layer(5) + [
        (ph_add_extra, {"src": "x3in"}),
        (emit_ln_inplace, {}),
    ], (("x3in", (D, T)),)))
    return gs


def kernel(**inputs):
    x = inputs["L0_attn_x"]
    w = _prep_weights(inputs)
    cur = [np.ascontiguousarray(x[b].T) for b in range(NCORES)]
    x3 = None
    for gi, (g, extra) in enumerate(_launch_groups()):
        phases = [(ph_load_x, {"src_name": "x_fm"})] + g + [(ph_store_out, {})]
        nc, in_names = build_program(phases, extra_inputs=extra)
        in_maps = []
        for b in range(NCORES):
            m = {"x_fm": cur[b]}
            if any(n == "x3in" for n, _ in extra):
                m["x3in"] = x3[b]
            m.update({k: w[k] for k in in_names if k in w})
            in_maps.append(m)
        import time as _time
        t0 = _time.time()
        last = None
        for attempt in range(3):
            try:
                res = run_bass_kernel_spmd(nc, in_maps,
                                           core_ids=list(range(NCORES)))
                cur = [np.ascontiguousarray(np.asarray(r["out"]))
                       for r in res.results]
                break
            except Exception as ex:
                last = ex
                print(f"[kernel] launch {gi} attempt {attempt} failed: "
                      f"{type(ex).__name__} after {_time.time()-t0:.0f}s",
                      flush=True)
        else:
            raise last
        print(f"[kernel] launch {gi} ok in {_time.time()-t0:.0f}s", flush=True)
        if gi == 2:
            x3 = cur
    return np.stack([np.ascontiguousarray(c.T) for c in cur])


# version 06 (final)
# speedup vs baseline: 1.0772x; 1.0772x over previous
import contextlib
import math
import numpy as np
import ml_dtypes
import concourse.bass as bass
import concourse.mybir as mybir
from concourse.bass_utils import run_bass_kernel_spmd

F32 = mybir.dt.float32
BF16 = mybir.dt.bfloat16
AF = mybir.ActivationFunctionType
ALU = mybir.AluOpType
AX = mybir.AxisListType

T = 4096
D = 1024
NCH = 8
TT = 512
NTT = T // TT
CW = T + 1
NCORES = 8
NEG = -1.0e30
EPS_LN = 1e-5

BF = ml_dtypes.bfloat16


def _host_constants():
    c = {}
    c["ident"] = np.eye(128, dtype=BF)
    c["ident_f32"] = np.eye(128, dtype=np.float32)
    c["ones_col_bf"] = np.ones((128, 1), dtype=BF)
    c["ones_row_f32"] = np.ones((1, 128), dtype=np.float32)
    c["ones_row_bf"] = np.ones((1, 128), dtype=BF)

    dm = np.zeros((4, 128, 512), dtype=np.float32)
    r = np.arange(128)[:, None]
    j = np.arange(512)[None, :]
    for p in range(4):
        dm[p] = np.where(j <= p * 128 + r, 0.0, NEG)
    c["diagmask"] = dm.astype(BF)

    lm = np.zeros((2, 128, 512), dtype=np.float32)
    j2 = np.arange(512)[None, :]
    for p in range(2):
        i = p * 128 + r
        lm[p] = np.where((j2 >= i) & (j2 <= i + 256), 0.0, NEG)
    c["locmask"] = lm.astype(BF)
    lm0 = np.zeros((2, 128, 256), dtype=np.float32)
    j3 = np.arange(256)[None, :]
    for p in range(2):
        lm0[p] = np.where(j3 <= p * 128 + r, 0.0, NEG)
    c["locmask0"] = lm0.astype(BF)

    sgn = np.ones((64, 1))
    sgn[0:32] = -1.0
    invf = (1.0 / 10000.0 ** (np.arange(0, 64, 2, dtype=np.float32) / 64.0)).astype(np.float64)
    pos = np.arange(T, dtype=np.float64)
    ang = pos[None, :] * invf[(np.arange(64) % 32)][:, None]
    c["l0_cos"] = np.cos(ang).astype(BF)
    c["l0_sin"] = (np.sin(ang) * sgn).astype(BF)

    half = np.arange(0, 64, 2, dtype=np.float64)
    invf2 = 1.0 / 10000.0 ** (half / 64.0)
    fr = pos[None, :] * invf2[(np.arange(64) % 32)][:, None]
    base = (half + 0.4 * 64.0) / (1.4 * 64.0)
    power = (pos - T // 2) / (256 // 2)
    scale = base[(np.arange(64) % 32)][:, None] ** power[None, :]
    c["xp_cq"] = (np.cos(fr) * scale).astype(BF)
    c["xp_sq"] = (np.sin(fr) * scale * sgn).astype(BF)
    c["xp_ck"] = (np.cos(fr) / scale).astype(BF)
    c["xp_sk"] = (np.sin(fr) / scale * sgn).astype(BF)

    sel = np.zeros((128, 2), dtype=BF)
    sel[0:64, 0] = 1
    sel[64:128, 1] = 1
    c["sel_ssq"] = sel
    selb = np.zeros((2, 128), dtype=BF)
    selb[0, 0:64] = 1
    selb[1, 64:128] = 1
    c["sel_bcast2"] = selb
    s8 = np.zeros((8, 8 * 128), dtype=BF)
    for e in range(8):
        s8[e, e * 128:(e + 1) * 128] = 1
    c["sel_bcast8"] = s8
    return c


def _l0_perm():
    p = np.zeros(D, dtype=np.int64)
    for h in range(16):
        for jj in range(64):
            src = h * 64 + (2 * jj if jj < 32 else 2 * (jj - 32) + 1)
            p[h * 64 + jj] = src
    return p


def _l1_qkv_perm():
    return np.arange(3072, dtype=np.int64)


def _prep_weights(inputs):
    w = {}
    perm = _l0_perm()
    w["l0_wq"] = np.ascontiguousarray(inputs["L0_attn_Wq"][:, perm])
    w["l0_wk"] = np.ascontiguousarray(inputs["L0_attn_Wk"][:, perm])
    w["l0_wv"] = np.ascontiguousarray(inputs["L0_attn_Wv"])
    w["l0_wo"] = np.ascontiguousarray(inputs["L0_attn_Wo"])
    w["l0_bo"] = np.ascontiguousarray(inputs["L0_attn_bo"].reshape(D, 1))
    w["moe_wg"] = np.ascontiguousarray(inputs["L0_ffn_w_gate"].T)
    w["moe_w1"] = np.ascontiguousarray(inputs["L0_ffn_W1"]).reshape(8 * D, 4096)
    w["moe_b1"] = np.ascontiguousarray(
        inputs["L0_ffn_b1"].reshape(8, 32, 128).transpose(0, 2, 1))
    w["moe_w2"] = np.ascontiguousarray(inputs["L0_ffn_W2"]).reshape(8 * 4096, D)
    w["moe_b2"] = np.ascontiguousarray(inputs["L0_ffn_b2"])
    w["mla_down"] = np.ascontiguousarray(inputs["L1_attn_w_kv_down"])
    w["mla_up"] = np.ascontiguousarray(inputs["L1_attn_w_kv_up"])
    w["mla_wq"] = np.ascontiguousarray(inputs["L1_attn_w_q"])
    w["mla_wo"] = np.ascontiguousarray(inputs["L1_attn_w_out"])

    for i in range(6):
        w[f"lf_wqkv_{i}"] = np.ascontiguousarray(inputs["L1_ffn_Wqkv"][i])
        w[f"lf_qs_{i}"] = np.ascontiguousarray(inputs["L1_ffn_q_scale"][i])
        w[f"lf_ks_{i}"] = np.ascontiguousarray(inputs["L1_ffn_k_scale"][i])
        w[f"lf_wo_{i}"] = np.ascontiguousarray(inputs["L1_ffn_Wo"][i])
    DI = 2730
    DIP = 2816
    W1 = inputs["L1_ffn_W1"]
    b1 = inputs["L1_ffn_b1"]
    W2 = inputs["L1_ffn_W2"]
    w1p = np.zeros((6, D, 2 * DIP), dtype=np.float32)
    b1p = np.zeros((6, 2 * DIP), dtype=np.float32)
    w2p = np.zeros((6, DIP, D), dtype=np.float32)
    for kk in range(DIP // 128):
        lo, hi = kk * 128, min((kk + 1) * 128, DI)
        n = hi - lo
        if n > 0:
            w1p[:, :, 2 * kk * 128: 2 * kk * 128 + n] = W1[:, :, lo:hi]
            w1p[:, :, (2 * kk + 1) * 128: (2 * kk + 1) * 128 + n] = W1[:, :, DI + lo: DI + hi]
            b1p[:, 2 * kk * 128: 2 * kk * 128 + n] = b1[:, lo:hi]
            b1p[:, (2 * kk + 1) * 128: (2 * kk + 1) * 128 + n] = b1[:, DI + lo: DI + hi]
            w2p[:, lo:hi, :] = W2[:, lo:hi, :]
    b1c = np.ascontiguousarray(b1p.reshape(6, 44, 128).transpose(0, 2, 1))
    gc = np.ascontiguousarray(inputs["L1_ffn_ln_g"].reshape(6, 8, 128).transpose(0, 2, 1))
    bc = np.ascontiguousarray(inputs["L1_ffn_ln_b"].reshape(6, 8, 128).transpose(0, 2, 1))
    b2c = np.ascontiguousarray(inputs["L1_ffn_b2"].reshape(6, 8, 128).transpose(0, 2, 1))
    for i in range(6):
        w[f"lf_w1_{i}"] = np.ascontiguousarray(w1p[i])
        w[f"lf_w2_{i}"] = np.ascontiguousarray(w2p[i])
        w[f"lf_b1c_{i}"] = np.ascontiguousarray(b1c[i])
        w[f"lf_gc_{i}"] = np.ascontiguousarray(gc[i])
        w[f"lf_bc_{i}"] = np.ascontiguousarray(bc[i])
        w[f"lf_b2c_{i}"] = np.ascontiguousarray(b2c[i])
    return w



ENGS = ("sync", "tensor", "vector", "scalar", "gpsimd")
N_DMA_SEMS = 24


class Prog:
    def __init__(self, nc, stack):
        self.nc = nc
        self.stack = stack
        self.gstack = stack
        self.streams = {e: [] for e in ENGS}
        self.cnt = {}
        self.sems = {}
        self.waited = {e: {} for e in ENGS}
        self.dma_rr = 0
        for name in ("Ete", "Eve", "Ese", "Egp", "Esy", "dout"):
            self._mksem(name)
        for i in range(N_DMA_SEMS):
            self._mksem(f"d{i}")
        for i in range(8):
            self._mksem(f"g{i}")
        for i in range(6):
            self._mksem(f"gl{i}")
        self.gdma_rr = 0
        self.glrr = 0
        self.consts = {}
        self.dram = {}
        self._constf = {}
        self.last_tok = {e: None for e in ENGS}

    def group_sem(self):
        s = f"gl{self.glrr % 6}"
        self.glrr += 1
        return s

    def constcol(self, value, dtype=F32):
        key = (dtype, float(value))
        if key not in self._constf:
            assert self.stack is self.gstack, (
                f"constcol({value}) first used inside a phase scope; "
                f"pre-register it in ph_const_load")
            npdt = np.float32 if dtype == F32 else BF
            data = np.full((128, 1), value, dtype=npdt)
            dh = self.nc.inline_tensor(data, f"cf{len(self._constf)}")
            t = self.gstack.enter_context(
                self.nc.sbuf_tensor(f"cfs{len(self._constf)}", [128, 1], dtype))
            tok = self.dma("sync", t[:], dh.ap())
            self._constf[key] = (t, tok)
        return self._constf[key]

    def _mksem(self, name):
        self.sems[name] = self.stack.enter_context(self.nc.semaphore(name))
        self.cnt[name] = 0

    def sb(self, name, shape, dtype):
        self._uid = getattr(self, "_uid", 0) + 1
        return self.stack.enter_context(
            self.nc.sbuf_tensor(f"{name}_u{self._uid}", shape, dtype))

    def ps(self, name, shape, dtype=F32):
        self._uid = getattr(self, "_uid", 0) + 1
        return self.stack.enter_context(
            self.nc.psum_tensor(f"{name}_u{self._uid}", shape, dtype))

    @contextlib.contextmanager
    def scope(self):
        with contextlib.ExitStack() as st:
            old = self.stack
            self.stack = st
            try:
                yield
            finally:
                self.stack = old

    def q(self, eng, fn, wait=(), inc=None, incn=1):
        wait = list(wait)
        if inc is None and eng in ("vector", "scalar", "gpsimd"):
            inc = {"vector": "Eve", "scalar": "Ese", "gpsimd": "Egp"}[eng]
        if inc is not None and self.cnt[inc] > 0:
            wait.append((inc, self.cnt[inc]))
        w = []
        for s, v in wait:
            if v <= 0:
                continue
            if self.waited[eng].get(s, 0) >= v:
                continue
            self.waited[eng][s] = v
            w.append((s, v))
        tok = None
        if inc is not None:
            self.cnt[inc] += incn
            tok = (inc, self.cnt[inc])
            self.last_tok[eng] = tok
        self.streams[eng].append((fn, tuple(w), inc, incn))
        return tok

    def te(self, fn, **kw):
        return self.q("tensor", fn, **kw)

    def ve(self, fn, **kw):
        return self.q("vector", fn, **kw)

    def se(self, fn, **kw):
        return self.q("scalar", fn, **kw)

    def gp(self, fn, **kw):
        return self.q("gpsimd", fn, **kw)

    def sy(self, fn, **kw):
        return self.q("sync", fn, **kw)

    def dma(self, queue, out_ap, in_ap, wait=(), sem=None, **dkw):
        if sem is None:
            if queue == "gpsimd":
                sem = f"g{self.gdma_rr % 8}"
                self.gdma_rr += 1
            else:
                sem = f"d{self.dma_rr % N_DMA_SEMS}"
                self.dma_rr += 1
        h = self.sems[sem]
        if queue == "gpsimd" and self.cnt[sem] > 0:
            wait = list(wait) + [(sem, self.cnt[sem])]
        def fn(e, o=out_ap, i=in_ap, kw2=dkw):
            return e.dma_start(out=o, in_=i, **kw2)
        return self.q(queue, fn, wait=wait, inc=sem, incn=16)

    def val(self, sem):
        return self.cnt[sem]

    def barrier(self):
        vals = [(s, self.cnt[s]) for s in self.sems if self.cnt[s] > 0]
        own = {"sync": "Esy", "tensor": "Ete", "vector": "Eve", "scalar": "Ese",
               "gpsimd": "Egp"}
        toks = {}
        for e in ENGS:
            lw = [self.last_tok[e]] if self.last_tok[e] else []
            toks[e] = self.q(e, lambda en: en.nop(), wait=lw, inc=own[e])
        for e in ENGS:
            w = [t for ee, t in toks.items() if ee != e] + vals
            self.q(e, lambda en: en.nop(), wait=w)

    def emit(self, which, e):
        import os
        noattach = bool(os.environ.get("KNOATTACH")) or which not in (
            "tensor", "sync")
        for fn, waits, inc, incn in self.streams[which]:
            head = waits if noattach else waits[:-1]
            for s, v in head:
                e.wait_ge(self.sems[s], v)
            inst = fn(e)
            if inst is None and (waits or inc is not None):
                inst = e.nop()
            if waits and inst is not None and not noattach:
                s, v = waits[-1]
                inst._wait_ge(self.sems[s], v)
            if inc is not None:
                inst.then_inc(self.sems[inc], incn)



def ph_const_load(P):
    nc = P.nc
    C = _host_constants()
    specs = {
        "ident": ([128, 128], BF16),
        "ones_col_bf": ([128, 1], BF16),
        "ones_row_f32": ([1, 128], F32),
        "ones_row_bf": ([1, 128], BF16),
        "sel_ssq": ([128, 2], BF16),
        "sel_bcast2": ([2, 128], BF16),
        "sel_bcast8": ([8, 8 * 128], BF16),
        "ident_f32": ([128, 128], F32),
    }
    flat = {
        "diagmask": C["diagmask"].transpose(1, 0, 2).reshape(128, 4 * 512),
        "locmask": C["locmask"].transpose(1, 0, 2).reshape(128, 2 * 512),
        "locmask0": C["locmask0"].transpose(1, 0, 2).reshape(128, 2 * 256),
    }
    for name, (shape, dt) in specs.items():
        data = flat.get(name, C[name])
        dh = nc.inline_tensor(np.ascontiguousarray(data), f"c_{name}")
        t = P.sb(f"sb_{name}", shape, dt)
        P.consts[name] = t
        P.dma("sync", t[:], dh.ap())
    for name in ("l0_cos", "l0_sin", "xp_cq", "xp_sq", "xp_ck", "xp_sk"):
        P.dram[name] = nc.inline_tensor(np.ascontiguousarray(C[name]), f"c_{name}")
    for name in ("diagmask", "locmask", "locmask0"):
        P.dram[name] = nc.inline_tensor(np.ascontiguousarray(flat[name]), f"c_{name}")
    for v in (EPS_LN, 0.125, 8.0, 1e-12, 1.0, -0.5, 1e-24):
        P.constcol(v)
    P.barrier()


def ph_load_x(P, src_name="x_fm"):
    x = P.dram[src_name]
    R = P.R
    for c in range(NCH):
        P.ve(lambda e, c=c: e.memset(R[:, c * CW: c * CW + 1], 0.0))
        P.dma("sync", R[:, c * CW + 1: (c + 1) * CW], x.ap()[c * 128:(c + 1) * 128, :])
    P.barrier()


def ph_store_out(P, dst="out"):
    o = P.dram[dst]
    R = P.R
    toks = []
    for c in range(NCH):
        toks.append(P.dma("sync", o.ap()[c * 128:(c + 1) * 128, :],
                          R[:, c * CW + 1: (c + 1) * CW], sem="dout"))
    P.sy(lambda e: e.nop(), wait=[toks[-1]])


def ph_store_R_dbg(P, dst):
    ph_store_out(P, dst)
    P.barrier()


def emit_ln_inplace(P):
    nc = P.nc
    R = P.R
    ones_bf = P.consts["ones_col_bf"]
    ones_row = P.consts["ones_row_f32"]
    with P.scope():
        xb = [P.sb(f"ln_xb{i}", [128, TT], BF16) for i in range(2)]
        sq = [P.sb(f"ln_sq{i}", [128, TT], BF16) for i in range(2)]
        rowbuf = P.sb("ln_rows", [1, 6 * TT], F32)
        sP = P.ps("ln_s", [1, TT])
        qP = P.ps("ln_q", [1, TT])
        rbP = P.ps("ln_rb", [128, TT])
        mbP = P.ps("ln_mb", [128, TT])
        mean = rowbuf[:, 0 * TT:1 * TT]
        ex2 = rowbuf[:, 1 * TT:2 * TT]
        var = rowbuf[:, 2 * TT:3 * TT]
        sd = rowbuf[:, 3 * TT:4 * TT]
        rstd = rowbuf[:, 4 * TT:5 * TT]
        mr = rowbuf[:, 5 * TT:6 * TT]

        mm_toks = [None, None]
        for tt in range(NTT):
            base = 1 + tt * TT
            for c in range(NCH):
                sl = R[:, c * CW + base: c * CW + base + TT]
                par = c % 2
                guard = [mm_toks[par]] if mm_toks[par] else []
                tcast = P.ve(lambda e, sl=sl, b=xb[par]: e.tensor_copy(b[:], sl),
                             wait=guard, inc="Eve")
                tsq = P.se(lambda e, sl=sl, b=sq[par]: e.activation(b[:], sl, AF.Square),
                           wait=guard, inc="Ese")
                P.te(lambda e, b=xb[par], c=c: e.matmul(
                    sP[:], ones_bf[:], b[:], start=(c == 0), stop=(c == NCH - 1)),
                    wait=[tcast])
                tk = P.te(lambda e, b=sq[par], c=c: e.matmul(
                    qP[:], ones_bf[:], b[:], start=(c == 0), stop=(c == NCH - 1)),
                    wait=[tsq], inc="Ete")
                mm_toks[par] = tk
                if c == NCH - 1:
                    mm_done = tk
            t1 = P.ve(lambda e: e.tensor_scalar_mul(mean, sP[:], 1.0 / D),
                      wait=[mm_done], inc="Eve")
            t2 = P.ve(lambda e: e.tensor_scalar_mul(ex2, qP[:], 1.0 / D), inc="Eve")
            t3 = P.ve(lambda e: e.tensor_mul(var, mean, mean), wait=[t1, t2], inc="Eve")
            tv = P.ve(lambda e: e.tensor_sub(var, ex2, var), wait=[t3], inc="Eve")
            epst, epstok = P.constcol(EPS_LN)
            nhalf, nhtok = P.constcol(-0.5)
            tln = P.se(lambda e: e.activation(sd, var, AF.Ln, bias=epst[0:1, :]),
                       wait=[tv, epstok, nhtok], inc="Ese")
            ts = P.se(lambda e: e.activation(rstd, sd, AF.Exp, scale=nhalf[0:1, :]),
                      wait=[tln], inc="Ese")
            tmr = P.ve(lambda e: e.tensor_mul(mr, mean, rstd), wait=[ts], inc="Eve")
            tb1 = P.te(lambda e: e.matmul(rbP[:], ones_row[:], rstd, start=True,
                                          stop=True), wait=[tmr], inc="Ete")
            tb2 = P.te(lambda e: e.matmul(mbP[:], ones_row[:], mr, start=True,
                                          stop=True), inc="Ete")
            for c in range(NCH):
                sl = R[:, c * CW + base: c * CW + base + TT]
                P.ve(lambda e, sl=sl: e.tensor_mul(sl, sl, rbP[:]), wait=[tb1, tb2])
                tlast = P.ve(lambda e, sl=sl: e.tensor_sub(sl, sl, mbP[:]),
                             inc="Eve" if c == NCH - 1 else None)
            P.te(lambda e: e.nop(), wait=[tlast])
    P.barrier()



def _load_w_sbuf(P, wsb, dram_name, rows, cols, idx=None):
    W = P.dram[dram_name].ap()
    if idx is not None:
        W = W[idx]
    sem = P.group_sem()
    tok = None
    for rc in range(rows // 128):
        tok = P.dma("gpsimd", wsb[:, rc * cols:(rc + 1) * cols],
                    W[rc * 128:(rc + 1) * 128, :], sem=sem)
    return [tok]


def _wslice(wsb, cols, rc, oc, width=128):
    return wsb[:, rc * cols + oc * width: rc * cols + oc * width + width]


def ph_l0_proj(P):
    R = P.R
    with P.scope():
        wq = P.sb("wq", [128, 8 * D], BF16)
        wk = P.sb("wk", [128, 8 * D], BF16)
        wv = P.sb("wv", [128, 8 * D], BF16)
        wtoks = (_load_w_sbuf(P, wq, "l0_wq", D, D) + _load_w_sbuf(P, wk, "l0_wk", D, D)
                 + _load_w_sbuf(P, wv, "l0_wv", D, D))
        emit_qkv_from_R(P, wq=wq, wk=wk, wv=wv, wtoks=wtoks, shift=False)
    P.barrier()


def emit_qkv_from_R(P, wq, wk, wv, wtoks, shift=False, qs_b=None, ks_b=None,
                    l2norm=False, rq_tm=None):
    R = P.R
    xst = [P.sb(f"pj_x{i}", [128, TT], BF16) for i in range(8)]
    ev = [P.sb(f"pj_e{i}", [128, TT], BF16) for i in range(4)]
    pq = [P.ps(f"pj_p{i}", [128, TT]) for i in range(3)]
    q_sp, k_sp, v_sp = P.dram["q_sp"], P.dram["k_sp"], P.dram["v_sp"]
    last_mm = [None] * 8
    ev_tok = [None] * 4
    pq_tok = [None] * 3
    pqi = 0
    evi = 0
    for tt in range(NTT):
        base = 1 + tt * TT
        casts = []
        for c in range(NCH):
            off = 0 if (shift and c >= NCH // 2) else 1
            sl = R[:, c * CW + base - 1 + off: c * CW + base - 1 + off + TT]
            g = [last_mm[c]] if last_mm[c] else []
            casts.append(P.ve(lambda e, sl=sl, b=xst[c]: e.tensor_copy(b[:], sl),
                              wait=g, inc="Eve"))
        allcast = casts[-1]
        for which, wsb, sp in (("q", wq, q_sp), ("k", wk, k_sp)):
            for oc in range(8):
                pp = pq[pqi % 3]
                gw = [pq_tok[pqi % 3]] if pq_tok[pqi % 3] else []
                for dc in range(NCH):
                    tk = P.te(lambda e, pp=pp, w=_wslice(wsb, D, dc, oc),
                              b=xst[dc], dc=dc: e.matmul(
                                  pp[:], w, b[:], start=(dc == 0), stop=(dc == NCH - 1)),
                              wait=([casts[dc]] + wtoks + gw) if dc == 0 else [casts[dc]],
                              inc="Ete" if dc == NCH - 1 else None)
                mmdone = tk
                last_mm_update = tk
                for dc in range(NCH):
                    last_mm[dc] = tk
                eb = ev[evi % 4]
                ge = [ev_tok[evi % 4]] if ev_tok[evi % 4] else []
                tev = P.se(lambda e, pp=pp, b=eb: e.copy(b[:], pp[:]),
                           wait=[mmdone] + ge, inc="Ese")
                pq_tok[pqi % 3] = tev
                ev_tok[evi % 4] = P.dma(
                    "sync", sp.ap()[oc * 128:(oc + 1) * 128, tt * TT:(tt + 1) * TT],
                    eb[:], wait=[tev])
                pqi += 1
                evi += 1
        for tc in range(4):
            for half in range(2):
                pp = pq[pqi % 3]
                gw = [pq_tok[pqi % 3]] if pq_tok[pqi % 3] else []
                for dc in range(NCH):
                    tk = P.te(lambda e, pp=pp, lh=xst[dc], tc=tc, half=half, dc=dc:
                              e.matmul(pp[:], lh[:, tc * 128:(tc + 1) * 128],
                                       wv[:, dc * D + half * TT: dc * D + half * TT + TT],
                                       start=(dc == 0), stop=(dc == NCH - 1)),
                              wait=([allcast] + gw) if dc == 0 else [],
                              inc="Ete" if dc == NCH - 1 else None)
                mmdone = tk
                for dc in range(NCH):
                    last_mm[dc] = tk
                eb = ev[evi % 4]
                ge = [ev_tok[evi % 4]] if ev_tok[evi % 4] else []
                tev = P.se(lambda e, pp=pp, b=eb: e.copy(b[:], pp[:]),
                           wait=[mmdone] + ge, inc="Ese")
                pq_tok[pqi % 3] = tev
                ev_tok[evi % 4] = P.dma(
                    "sync",
                    v_sp.ap()[tt * 4 + tc, :, half * TT:(half + 1) * TT],
                    eb[:], wait=[tev])
                pqi += 1
                evi += 1


def emit_causal_attention(P, rope=False, scale=0.125):
    ST = 2048
    with P.scope():
        vq = P.sb("at_vq", [128, 32 * 128], BF16)
        qh = P.sb("at_qh", [64, T], BF16)
        kh = P.sb("at_kh", [64, T], BF16)
        Pb = P.sb("at_P", [128, T], BF16)
        oh = P.sb("at_oh", [64, T], BF16)
        pts = P.sb("at_pts", [128, TT], BF16)
        cols = P.sb("at_cols", [128, 8], F32)
        ctab = P.sb("at_ctab", [64, T], BF16) if rope else None
        stab = P.sb("at_stab", [64, T], BF16) if rope else None
        swld = P.sb("at_swld", [64, T], BF16) if rope else None
        S = P.ps("at_S", [128, ST])
        PT = P.ps("at_PT", [128, TT], BF16)
        OT = P.ps("at_OT", [64, 128])
        ident = P.consts["ident"]
        dmask = P.sb("at_dmask", [128, 4 * 512], BF16)
        dm_tok = P.dma("sync", dmask[:], P.dram["diagmask"].ap())
        sc_col, sc_tok = P.constcol(scale)
        q_sp, k_sp, v_sp, o_sp = (P.dram[n] for n in ("q_sp", "k_sp", "v_sp", "o_sp"))
        mneg0 = cols[:, 0:1]
        mneg1 = cols[:, 1:2]
        bias = cols[:, 2:3]
        l0c = cols[:, 3:4]
        l1c = cols[:, 4:5]
        fc = cols[:, 5:6]
        rc_ = cols[:, 6:7]
        tmpc = cols[:, 7:8]

        if rope:
            ttok = [P.dma("sync", ctab[:], P.dram["l0_cos"].ap()),
                    P.dma("sync", stab[:], P.dram["l0_sin"].ap())]
        else:
            ttok = []

        vq_tok = None
        qk_free = None
        oh_tok = None
        prev_exp = None
        prev_oev = None
        prev_av = None
        pt_guard = None
        for h in range(16):
            if h % 2 == 0:
                gw = [t for t in (prev_av,) if t]
                sem = P.group_sem()
                for j in range(32):
                    vt = P.dma(
                        "sync", vq[:, j * 128:(j + 1) * 128,],
                        P.dram["v_sp"].ap()[j, :, (h // 2) * 128:(h // 2 + 1) * 128],
                        wait=gw if j == 0 else [], sem=sem)
                vq_toks = [vt]
            gw = [t for t in (qk_free,) if t]
            tq = P.dma("sync", qh[:], q_sp.ap()[h * 64:(h + 1) * 64, :], wait=gw)
            tk_ = P.dma("sync", kh[:], k_sp.ap()[h * 64:(h + 1) * 64, :], wait=gw)
            rope_done = []
            if rope:
                swfree = None
                for src, sp_, ldt in ((qh, q_sp, tq), (kh, k_sp, tk_)):
                    T1 = Pb[0:64, :]
                    ds1 = P.dma("sync", swld[0:32, :],
                                sp_.ap()[h * 64 + 32: h * 64 + 64, :],
                                wait=[swfree] if swfree else [])
                    ds2 = P.dma("sync", swld[32:64, :],
                                sp_.ap()[h * 64: h * 64 + 32, :])
                    a1 = P.ve(lambda e, s=src: e.tensor_mul(T1, s[:], ctab[:]),
                              wait=[ldt] + ttok + [t for t in (prev_av,) if t]
                              + rope_done[-1:], inc="Eve")
                    a2 = P.ve(lambda e, s=src: e.tensor_mul(s[:], swld[:], stab[:]),
                              wait=[a1, ds1, ds2], inc="Eve")
                    swfree = a2
                    rope_done.append(P.ve(lambda e, s=src: e.tensor_add(
                        s[:], s[:], T1), wait=[a2], inc="Eve"))
                ldw = list(rope_done)
            else:
                ldw = [tq, tk_]

            pnorm_tok = None
            for c in range(32):
                klen = (c + 1) * 128
                ns = (klen + ST - 1) // ST
                for s in range(ns):
                    w = min(ST, klen - s * ST)
                    ntile = (w + TT - 1) // TT
                    for kt in range(ntile):
                        n = min(TT, w - kt * TT)
                        isdiag = (s == ns - 1 and kt == ntile - 1)
                        islastmm = (kt == ntile - 1) and not isdiag
                        tkk = P.te(lambda e, c=c, s=s, kt=kt, n=n, isdiag=isdiag:
                                   e.matmul(
                                       S[:, kt * TT: kt * TT + n],
                                       qh[:, c * 128:(c + 1) * 128],
                                       kh[:, s * ST + kt * TT: s * ST + kt * TT + n],
                                       start=True, stop=not isdiag),
                                   wait=(ldw + [t for t in (prev_exp,) if t])
                                   if kt == 0 else [],
                                   inc="Ete" if islastmm else None)
                        if isdiag:
                            tkk = P.te(lambda e, c=c, kt=kt, n=n: e.matmul(
                                S[:, kt * TT: kt * TT + n], ident[:],
                                dmask[:, (c % 4) * TT: (c % 4) * TT + n],
                                start=False, stop=True),
                                wait=[dm_tok], inc="Ete")
                    qk_done = tkk
                    mn = mneg0 if s == 0 else mneg1
                    tm = P.ve(lambda e, mn=mn, w=w: e.reduce_max(
                        mn, S[:, 0:w], axis=AX.X, negate=True),
                        wait=[qk_done], inc="Eve")
                    if s == 0:
                        tb = P.ve(lambda e: e.tensor_mul(bias, mneg0, sc_col[:]),
                                  wait=[tm, sc_tok], inc="Eve")
                        lcur = l0c
                    else:
                        t1 = P.ve(lambda e: e.tensor_tensor(
                            tmpc, mneg0, mneg1, ALU.min), wait=[tm], inc="Eve")
                        td = P.ve(lambda e: e.tensor_sub(fc, tmpc, mneg0),
                                  wait=[t1], inc="Eve")
                        td2 = P.ve(lambda e: e.tensor_mul(fc, fc, sc_col[:]),
                                   wait=[td], inc="Eve")
                        tb = P.ve(lambda e: e.tensor_mul(bias, tmpc, sc_col[:]),
                                  wait=[t1], inc="Eve")
                        tf = P.se(lambda e: e.activation(fc, fc, AF.Exp),
                                  wait=[td2], inc="Ese")
                        lcur = l1c
                    texp = P.se(lambda e, s=s, w=w, lc=lcur: e.activation(
                        Pb[:, s * ST: s * ST + w], S[:, 0:w], AF.Exp,
                        bias=bias, scale=sc_col[:], accum_out=lc),
                        wait=[tb], inc="Ese")
                    prev_exp = texp
                if ns == 1:
                    tr0 = P.ve(lambda e: e.reciprocal(rc_, l0c), wait=[prev_exp],
                               inc="Eve")
                    tnorm = P.ve(lambda e, klen=klen: e.tensor_scalar_mul(
                        Pb[:, 0:klen], Pb[:, 0:klen], rc_), wait=[tr0], inc="Eve")
                else:
                    tp0 = P.ve(lambda e: e.tensor_scalar_mul(
                        Pb[:, 0:ST], Pb[:, 0:ST], fc), wait=[tf, prev_exp], inc="Eve")
                    tl0 = P.ve(lambda e: e.tensor_mul(l0c, l0c, fc), wait=[tf, prev_exp],
                               inc="Eve")
                    tl1 = P.ve(lambda e: e.tensor_add(l0c, l0c, l1c), wait=[tl0],
                               inc="Eve")
                    tr0 = P.ve(lambda e: e.reciprocal(rc_, l0c), wait=[tl1], inc="Eve")
                    tnorm = P.ve(lambda e, klen=klen: e.tensor_scalar_mul(
                        Pb[:, 0:klen], Pb[:, 0:klen], rc_), wait=[tr0, tp0],
                        inc="Eve")
                pnorm_tok = tnorm
                ngr = (klen + TT - 1) // TT
                for g in range(ngr):
                    gw2 = min(TT, klen - g * TT)
                    nt = gw2 // 128
                    for jj in range(nt):
                        tpt = P.te(lambda e, g=g, jj=jj: e.transpose(
                            PT[:, jj * 128:(jj + 1) * 128],
                            Pb[:, g * TT + jj * 128: g * TT + (jj + 1) * 128],
                            ident[:]),
                            wait=([tnorm, pt_guard] if jj == 0 and pt_guard
                                  else ([tnorm] if jj == 0 else [])),
                            inc="Ete" if jj == nt - 1 else None)
                    tev = P.se(lambda e, gw2=gw2: e.copy(pts[:, 0:gw2], PT[:, 0:gw2]),
                               wait=[tpt], inc="Ese")
                    pt_guard = tev
                    for jj in range(nt):
                        gchunk = g * 4 + jj
                        first = (g == 0 and jj == 0)
                        last = (g == ngr - 1 and jj == nt - 1)
                        wv_ = []
                        if jj == 0:
                            wv_ = [tev] + vq_toks
                            if first and prev_oev:
                                wv_.append(prev_oev)
                        tav = P.te(lambda e, gchunk=gchunk, jj=jj, h=h, first=first,
                                   last=last: e.matmul(
                                       OT[:, :],
                                       vq[:, gchunk * 128 + (h % 2) * 64:
                                          gchunk * 128 + (h % 2) * 64 + 64],
                                       pts[:, jj * 128:(jj + 1) * 128],
                                       start=first, stop=last),
                                   wait=wv_,
                                   inc="Ete" if last else None)
                av_done = tav
                wev = [av_done]
                if c == 0 and oh_tok:
                    wev.append(oh_tok)
                toev = P.se(lambda e, c=c: e.copy(oh[:, c * 128:(c + 1) * 128], OT[:, :]),
                            wait=wev, inc="Ese")
                prev_oev = toev
            qk_free = pnorm_tok
            prev_av = av_done
            oh_tok = P.dma("sync", o_sp.ap()[h * 64:(h + 1) * 64, :], oh[:],
                           wait=[prev_oev])
        P.sy(lambda e: e.nop(), wait=[oh_tok])
    P.barrier()


def ph_out_proj(P, wname, bias_name=None, nin=D, idx=None):
    R = P.R
    with P.scope():
        wsb = P.sb("op_w", [128, (nin // 128) * D], BF16)
        wtoks = _load_w_sbuf(P, wsb, wname, nin, D, idx=idx)
        bo = P.sb("op_b", [128, 8], F32) if bias_name else None
        btoks = []
        if bias_name:
            for oc in range(8):
                btoks.append(P.dma("sync", bo[:, oc:oc + 1],
                                   P.dram[bias_name].ap()[oc * 128:(oc + 1) * 128, :]))
        ost = [P.sb(f"op_o{i}", [128, TT], BF16) for i in range(nin // 128)]
        pq = [P.ps(f"op_p{i}", [128, TT]) for i in range(2)]
        o_sp = P.dram["o_sp"]
        pq_tok = [None, None]
        ld_tok = [None] * (nin // 128)
        mm_last = [None] * (nin // 128)
        for tt in range(NTT):
            lds = []
            for dc in range(nin // 128):
                g = [mm_last[dc]] if mm_last[dc] else []
                ld_tok[dc] = P.dma("sync", ost[dc][:],
                                   o_sp.ap()[dc * 128:(dc + 1) * 128,
                                             tt * TT:(tt + 1) * TT], wait=g)
                lds.append(ld_tok[dc])
            for oc in range(8):
                pp = pq[oc % 2]
                gw = [pq_tok[oc % 2]] if pq_tok[oc % 2] else []
                for dc in range(nin // 128):
                    tk = P.te(lambda e, pp=pp, w=_wslice(wsb, D, dc, oc), b=ost[dc],
                              dc=dc, n=nin // 128: e.matmul(
                                  pp[:], w, b[:], start=(dc == 0), stop=(dc == n - 1)),
                              wait=([lds[dc]] + wtoks + btoks + gw) if dc == 0 else [lds[dc]],
                              inc="Ete" if dc == nin // 128 - 1 else None)
                for dc in range(nin // 128):
                    mm_last[dc] = tk
                sl = R[:, oc * CW + 1 + tt * TT: oc * CW + 1 + tt * TT + TT]
                if bias_name:
                    tadd = P.ve(lambda e, pp=pp, sl=sl, oc=oc: e.scalar_tensor_tensor(
                        sl, pp[:], bo[:, oc:oc + 1], sl, ALU.add, ALU.add),
                        wait=[tk], inc="Eve")
                else:
                    tadd = P.ve(lambda e, pp=pp, sl=sl: e.tensor_add(sl, sl, pp[:]),
                                wait=[tk], inc="Eve")
                pq_tok[oc % 2] = tadd
    P.barrier()



def ph_lf_save_x3(P):
    x3 = P.dram["x3_sp"]
    R = P.R
    for c in range(NCH):
        P.dma("sync", x3.ap()[c * 128:(c + 1) * 128, :],
              R[:, c * CW + 1:(c + 1) * CW])
    P.barrier()


def ph_final_x3_ln(P):
    x3 = P.dram["x3_sp"]
    R = P.R
    with P.scope():
        xt = P.sb("fx_t", [128, T], F32)
        for c in range(NCH):
            tok = P.dma("sync", xt[:], x3.ap()[c * 128:(c + 1) * 128, :])
            P.ve(lambda e, c=c: e.tensor_add(
                R[:, c * CW + 1:(c + 1) * CW], R[:, c * CW + 1:(c + 1) * CW],
                xt[:]), wait=[tok])
            tok2 = P.last_tok["vector"]
            P.q("sync", lambda e: e.nop(), wait=[tok2])
    P.barrier()
    emit_ln_inplace(P)


def ph_lf_qkv(P, i):
    R = P.R
    sel_ssq = P.consts["sel_ssq"]
    selb2 = P.consts["sel_bcast2"]
    rq_tm = P.glob["rq_tm"]
    wq = P.dram[f"lf_wqkv_{i}"]
    with P.scope():
        wvsb = P.sb("lq_wv", [128, 8 * 1024], BF16)
        slab = [P.sb(f"lq_s{j}", [128, 8 * 128], BF16) for j in range(2)]
        xst = [P.sb(f"lq_x{j}", [128, TT], BF16) for j in range(8)]
        ev = [P.sb(f"lq_e{j}", [128, TT], BF16) for j in range(4)]
        sqb = P.sb("lq_sq", [128, TT], BF16)
        rr = P.sb("lq_rr", [2, 2 * TT], F32)
        rkbf = P.sb("lq_rk", [2, TT], BF16)
        ksc = P.sb("lq_ks", [128, TT], BF16)
        pq = [P.ps(f"lq_p{j}", [128, TT]) for j in range(2)]
        ssqP = P.ps("lq_ssq", [2, TT])
        auxP = P.ps("lq_aux", [128, TT])
        sem_wv = P.group_sem()
        for dc in range(NCH):
            wvtok = P.dma("gpsimd", wvsb[:, dc * 1024:(dc + 1) * 1024],
                          wq.ap()[dc * 128:(dc + 1) * 128, 2048:3072], sem=sem_wv)
        wvtoks = [wvtok]
        wqr = wq.ap().rearrange("(dc p) n -> p dc n", p=128)
        q_sp, k_sp, v_sp = P.dram["q_sp"], P.dram["k_sp"], P.dram["v_sp"]
        epst24, _ = P.constcol(1e-24)
        nhalf, _ = P.constcol(-0.5)
        last_mm = [None] * 8
        ev_tok = [None] * 4
        pq_tok = [None, None]
        slab_tok = [None, None]
        slab_mm = [None, None]
        aux_free = None
        ssq_free = None
        ksc_dma = None
        evi = 0
        for tt in range(NTT):
            base = 1 + tt * TT
            casts = []
            for c in range(NCH):
                off = 0 if c >= NCH // 2 else 1
                sl = R[:, c * CW + base - 1 + off: c * CW + base - 1 + off + TT]
                g = [last_mm[c]] if last_mm[c] else []
                casts.append(P.ve(lambda e, sl=sl, b=xst[c]: e.tensor_copy(b[:], sl),
                                  wait=g))
            for oc in range(16):
                par = oc % 2
                gm = [slab_mm[par]] if slab_mm[par] else []
                slab_tok[par] = P.dma("gpsimd", slab[par][:],
                                      wqr[:, :, oc * 128:(oc + 1) * 128], wait=gm)
                pp = pq[par]
                gw = [pq_tok[par]] if pq_tok[par] else []
                for dc in range(NCH):
                    tk = P.te(lambda e, pp=pp, par=par, dc=dc: e.matmul(
                        pp[:], slab[par][:, dc * 128:(dc + 1) * 128], xst[dc][:],
                        start=(dc == 0), stop=(dc == NCH - 1)),
                        wait=([casts[dc], slab_tok[par]] + gw) if dc == 0
                        else [casts[dc]],
                        inc="Ete" if dc == NCH - 1 else None)
                slab_mm[par] = tk
                for dc in range(NCH):
                    last_mm[dc] = tk
                eb = ev[evi % 4]
                ge = [ev_tok[evi % 4]] if ev_tok[evi % 4] else []
                tev = P.se(lambda e, pp=pp, b=eb: e.copy(b[:], pp[:]),
                           wait=[tk] + ge, inc="Ese")
                tsq = P.se(lambda e, pp=pp: e.activation(sqb[:], pp[:], AF.Square),
                           wait=[tk])
                pq_tok[par] = tsq
                tss = P.te(lambda e: e.matmul(ssqP[:], sel_ssq[:], sqb[:],
                                              start=True, stop=True),
                           wait=[tsq] + ([ssq_free] if ssq_free else []), inc="Ete")
                ss = rr[:, 0:TT]
                rstd2 = rr[:, TT:2 * TT]
                tc1 = P.ve(lambda e: e.tensor_scalar_max(ss, ssqP[:], 1e-24),
                           wait=[tss])
                ssq_free = tc1
                tc2 = P.se(lambda e: e.activation(rstd2, ss, AF.Ln), wait=[tc1])
                tc3 = P.se(lambda e: e.activation(rstd2, rstd2, AF.Exp,
                                                  scale=nhalf[0:2, :]), wait=[tc2])
                if oc < 8:
                    ev_tok[evi % 4] = P.dma(
                        "sync", q_sp.ap()[oc * 128:(oc + 1) * 128,
                                          tt * TT:(tt + 1) * TT],
                        eb[:], wait=[tev])
                    for tc in range(4):
                        ttr = P.te(lambda e, tc=tc: e.transpose(
                            auxP[0:128, tc * 2: tc * 2 + 2],
                            rstd2[:, tc * 128:(tc + 1) * 128],
                            P.consts["ident_f32"][0:2, 0:2]),
                            wait=[tc3] + ([aux_free] if aux_free and tc == 0
                                          else []),
                            inc="Ete")
                        P.se(lambda e, tc=tc, oc=oc, tt=tt: e.copy(
                            rq_tm[:, (tt * 4 + tc) * 16 + 2 * oc:
                                  (tt * 4 + tc) * 16 + 2 * oc + 2],
                            auxP[0:128, tc * 2: tc * 2 + 2]), wait=[ttr])
                    aux_free = P.last_tok["scalar"]
                else:
                    tc4 = P.ve(lambda e: e.tensor_copy(rkbf[:], rstd2), wait=[tc3])
                    tbc = P.te(lambda e: e.matmul(auxP[:], selb2[:], rkbf[:],
                                                  start=True, stop=True),
                               wait=[tc4] + ([aux_free] if aux_free else []),
                               inc="Ete")
                    tks = P.ve(lambda e, b=eb: e.tensor_mul(ksc[:], b[:], auxP[:]),
                               wait=[tbc, tev]
                               + ([ksc_dma] if ksc_dma else []))
                    aux_free = tks
                    ev_tok[evi % 4] = tks
                    ksc_dma = P.dma(
                        "sync", k_sp.ap()[(oc - 8) * 128:(oc - 7) * 128,
                                          tt * TT:(tt + 1) * TT],
                        ksc[:], wait=[tks])
                evi += 1
            for tc in range(4):
                for half in range(2):
                    par = tc % 2
                    pp = pq[par]
                    gw = [pq_tok[par]] if pq_tok[par] else []
                    for dc in range(NCH):
                        tk = P.te(lambda e, pp=pp, dc=dc, tc=tc, half=half: e.matmul(
                            pp[:], xst[dc][:, tc * 128:(tc + 1) * 128],
                            wvsb[:, dc * 1024 + half * TT: dc * 1024 + half * TT + TT],
                            start=(dc == 0), stop=(dc == NCH - 1)),
                            wait=(gw + wvtoks) if dc == 0 else [],
                            inc="Ete" if dc == NCH - 1 else None)
                    for dc in range(NCH):
                        last_mm[dc] = tk
                    eb = ev[evi % 4]
                    ge = [ev_tok[evi % 4]] if ev_tok[evi % 4] else []
                    tev = P.se(lambda e, pp=pp, b=eb: e.copy(b[:], pp[:]),
                               wait=[tk] + ge, inc="Ese")
                    pq_tok[par] = tev
                    ev_tok[evi % 4] = P.dma(
                        "sync", v_sp.ap()[tt * 4 + tc, :, half * TT:(half + 1) * TT],
                        eb[:], wait=[tev])
                    evi += 1
    P.barrier()


def ph_lf_rope(P, i, which):
    sp = P.dram["q_sp" if which == "q" else "k_sp"]
    ctn = "xp_cq" if which == "q" else "xp_ck"
    stn = "xp_sq" if which == "q" else "xp_sk"
    scn = f"lf_qs_{i}" if which == "q" else f"lf_ks_{i}"
    with P.scope():
        ctab = P.sb("lr_c", [64, T], BF16)
        stab = P.sb("lr_s", [64, T], BF16)
        Atab = P.sb("lr_A", [64, T], BF16)
        Btab = P.sb("lr_B", [64, T], BF16)
        qh = P.sb("lr_qh", [64, T], BF16)
        swld = P.sb("lr_sw", [64, T], BF16)
        T1 = P.sb("lr_T1", [64, T], BF16)
        scol = P.sb("lr_sc", [64, 2], F32)
        t1 = P.dma("sync", ctab[:], P.dram[ctn].ap())
        t2 = P.dma("sync", stab[:], P.dram[stn].ap())
        sc = P.dram[scn].ap().rearrange("(d o) -> d o", o=1)
        t3 = P.dma("sync", scol[:, 0:1], sc)
        t4 = P.dma("sync", scol[32:64, 1:2], sc[0:32])
        t5 = P.dma("sync", scol[0:32, 1:2], sc[32:64])
        ta = P.ve(lambda e: e.tensor_mul(
            Atab[:], ctab[:], scol[:, 0:1].broadcast_to([64, T])),
            wait=[t1, t3])
        tb = P.ve(lambda e: e.tensor_mul(
            Btab[:], stab[:], scol[:, 1:2].broadcast_to([64, T])),
            wait=[t2, t4, t5])
        dq = None
        swfree = None
        for h in range(16):
            g = [dq] if dq else []
            g2 = [swfree] if swfree else []
            l1 = P.dma("sync", qh[:], sp.ap()[h * 64:(h + 1) * 64, :], wait=g)
            l2 = P.dma("sync", swld[0:32, :],
                       sp.ap()[h * 64 + 32: h * 64 + 64, :], wait=g2)
            l3 = P.dma("sync", swld[32:64, :], sp.ap()[h * 64: h * 64 + 32, :],
                       wait=g2)
            a1 = P.ve(lambda e: e.tensor_mul(T1[:], qh[:], Atab[:]),
                      wait=[l1, ta, tb])
            a2 = P.ve(lambda e: e.tensor_mul(qh[:], swld[:], Btab[:]),
                      wait=[l2, l3, a1])
            swfree = a2
            a3 = P.ve(lambda e: e.tensor_add(qh[:], qh[:], T1[:]), wait=[a2])
            dq = P.dma("sync", sp.ap()[h * 64:(h + 1) * 64, :], qh[:], wait=[a3])
    P.barrier()


def ph_lf_attn(P, i):
    rq_tm = P.glob["rq_tm"]
    ident = P.consts["ident"]
    q_sp, k_sp, v_sp, o_sp = (P.dram[n] for n in ("q_sp", "k_sp", "v_sp", "o_sp"))
    with P.scope():
        vq = P.sb("la_vq", [128, 32 * 128], BF16)
        qh = P.sb("la_qh", [64, T], BF16)
        kh = P.sb("la_kh", [64, T], BF16)
        Pb = P.sb("la_P", [128, TT], BF16)
        oh = P.sb("la_oh", [64, T], BF16)
        pts = P.sb("la_pts", [128, TT], BF16)
        rq8 = P.sb("la_rq8", [128, 32 * 16], F32)
        lmask = P.sb("la_lm", [128, 2 * 512], BF16)
        lmask0 = P.sb("la_lm0", [128, 2 * 256], BF16)
        cols = P.sb("la_cols", [128, 4], F32)
        S = P.ps("la_S", [128, TT])
        PT = P.ps("la_PT", [128, TT], BF16)
        OT = P.ps("la_OT", [64, 128])
        mneg = cols[:, 0:1]
        bias = cols[:, 1:2]
        lc = cols[:, 2:3]
        rc_ = cols[:, 3:4]
        tm1 = P.dma("sync", lmask[:], P.dram["locmask"].ap())
        tm2 = P.dma("sync", lmask0[:], P.dram["locmask0"].ap())
        tr8 = P.ve(lambda e: e.tensor_scalar_mul(rq8[:], rq_tm[:], 8.0))
        prev_exp = None
        prev_oev = None
        prev_av = None
        pt_guard = None
        oh_tok = None
        qk_free = None
        vq_toks = []
        for h in range(16):
            if h % 2 == 0:
                gw = [t for t in (prev_av,) if t]
                sem = P.group_sem()
                for j in range(32):
                    vt = P.dma(
                        "sync", vq[:, j * 128:(j + 1) * 128],
                        v_sp.ap()[j, :, (h // 2) * 128:(h // 2 + 1) * 128],
                        wait=gw if j == 0 else [], sem=sem)
                vq_toks = [vt]
            gw = [qk_free] if qk_free else []
            tq = P.dma("sync", qh[:], q_sp.ap()[h * 64:(h + 1) * 64, :], wait=gw)
            tk_ = P.dma("sync", kh[:], k_sp.ap()[h * 64:(h + 1) * 64, :], wait=gw)
            for w in range(16):
                for p_ in range(2):
                    tc = 2 * w + p_
                    if w == 0:
                        kstart, kw = 0, 256
                        mk = lmask0[:, p_ * 256:(p_ + 1) * 256]
                    else:
                        kstart, kw = w * 256 - 256, 512
                        mk = lmask[:, p_ * 512:(p_ + 1) * 512]
                    t_qk = P.te(lambda e, tc=tc, kstart=kstart, kw=kw: e.matmul(
                        S[:, 0:kw], qh[:, tc * 128:(tc + 1) * 128],
                        kh[:, kstart: kstart + kw], start=True, stop=False),
                        wait=[tq, tk_] + ([prev_exp] if prev_exp else []))
                    t_qk = P.te(lambda e, mk=mk, kw=kw: e.matmul(
                        S[:, 0:kw], ident[:], mk, start=False, stop=True),
                        wait=[tm1, tm2], inc="Ete")
                    tm = P.ve(lambda e, kw=kw: e.reduce_max(
                        mneg, S[:, 0:kw], axis=AX.X, negate=True),
                        wait=[t_qk, tr8])
                    sccol = rq8[:, tc * 16 + h: tc * 16 + h + 1]
                    tb = P.ve(lambda e, sccol=sccol: e.tensor_mul(bias, mneg, sccol),
                              wait=[tm])
                    texp = P.se(lambda e, kw=kw, sccol=sccol: e.activation(
                        Pb[:, 0:kw], S[:, 0:kw], AF.Exp,
                        bias=bias, scale=sccol, accum_out=lc), wait=[tb])
                    prev_exp = texp
                    tr0 = P.ve(lambda e: e.reciprocal(rc_, lc), wait=[texp])
                    tnorm = P.ve(lambda e, kw=kw: e.tensor_scalar_mul(
                        Pb[:, 0:kw], Pb[:, 0:kw], rc_), wait=[tr0])
                    nt = kw // 128
                    for jj in range(nt):
                        tpt = P.te(lambda e, jj=jj: e.transpose(
                            PT[:, jj * 128:(jj + 1) * 128],
                            Pb[:, jj * 128:(jj + 1) * 128], ident[:]),
                            wait=([tnorm] + ([pt_guard] if pt_guard else []))
                            if jj == 0 else [],
                            inc="Ete" if jj == nt - 1 else None)
                    tev = P.se(lambda e, kw=kw: e.copy(pts[:, 0:kw], PT[:, 0:kw]),
                               wait=[tpt], inc="Ese")
                    pt_guard = tev
                    kc0 = kstart // 128
                    for jj in range(nt):
                        wv_ = []
                        if jj == 0:
                            wv_ = [tev] + vq_toks
                            if prev_oev:
                                wv_.append(prev_oev)
                        tav = P.te(lambda e, jj=jj, kc0=kc0, h=h, nt=nt: e.matmul(
                            OT[:, :],
                            vq[:, (kc0 + jj) * 128 + (h % 2) * 64:
                               (kc0 + jj) * 128 + (h % 2) * 64 + 64],
                            pts[:, jj * 128:(jj + 1) * 128],
                            start=(jj == 0), stop=(jj == nt - 1)),
                            wait=wv_, inc="Ete" if jj == nt - 1 else None)
                    prev_av = tav
                    wev = [tav]
                    if tc == 0 and oh_tok:
                        wev.append(oh_tok)
                    toev = P.se(lambda e, tc=tc: e.copy(
                        oh[:, tc * 128:(tc + 1) * 128], OT[:, :]), wait=wev,
                        inc="Ese")
                    prev_oev = toev
                    qk_free = tnorm
            oh_tok = P.dma("sync", o_sp.ap()[h * 64:(h + 1) * 64, :], oh[:],
                           wait=[prev_oev])
        P.sy(lambda e: e.nop(), wait=[oh_tok])
    P.barrier()


def ph_lf_geglu(P, i):
    R = P.R
    ones_bf = P.consts["ones_col_bf"]
    ones_row = P.consts["ones_row_f32"]
    w1 = P.dram[f"lf_w1_{i}"]
    w2 = P.dram[f"lf_w2_{i}"]
    NOC = 44
    NHC = 22
    with P.scope():
        xst = [P.sb(f"lg_x{j}", [128, TT], BF16) for j in range(2)]
        sqst = [P.sb(f"lg_q{j}", [128, TT], BF16) for j in range(2)]
        xnst = [P.sb(f"lg_n{j}", [128, TT], BF16) for j in range(8)]
        hbuf = P.sb("lg_h", [128, NHC * TT], BF16)
        abuf = P.sb("lg_a", [128, TT], BF16)
        gbuf = P.sb("lg_g", [128, TT], BF16)
        tmpf = P.sb("lg_t", [128, TT], F32)
        rowbuf = P.sb("lg_rows", [1, 6 * TT], F32)
        bias1 = P.sb("lg_b1", [128, NOC], F32)
        gcol = P.sb("lg_gc", [128, 8], F32)
        bcol = P.sb("lg_bc", [128, 8], F32)
        b2col = P.sb("lg_b2", [128, 8], F32)
        slab = [P.sb(f"lg_s{j}", [128, 8 * 128], BF16) for j in range(2)]
        slab2 = [P.sb(f"lg_s2{j}", [128, 1024], BF16) for j in range(2)]
        pool = P.ps("lg_pool", [128, 8 * TT])
        sP = pool[0:1, 4 * TT: 5 * TT]
        qP = pool[0:1, 5 * TT: 6 * TT]
        rbP = pool[:, 6 * TT: 7 * TT]
        mbP = pool[:, 7 * TT: 8 * TT]
        mean = rowbuf[:, 0:TT]
        ex2 = rowbuf[:, TT:2 * TT]
        var = rowbuf[:, 2 * TT:3 * TT]
        sd = rowbuf[:, 3 * TT:4 * TT]
        rstd = rowbuf[:, 4 * TT:5 * TT]
        mr = rowbuf[:, 5 * TT:6 * TT]
        bt = [P.dma("sync", bias1[:], P.dram[f"lf_b1c_{i}"].ap()),
              P.dma("sync", gcol[:], P.dram[f"lf_gc_{i}"].ap()),
              P.dma("sync", bcol[:], P.dram[f"lf_bc_{i}"].ap()),
              P.dma("sync", b2col[:], P.dram[f"lf_b2c_{i}"].ap())]
        w1r = w1.ap().rearrange("(dc p) n -> p dc n", p=128)
        epst, _ = P.constcol(EPS_LN)
        nhalf, _ = P.constcol(-0.5)
        stat_guard = None
        pool_y_free = None
        slab_tok = [None, None]
        slab_mm = [None, None]
        slab2_tok = [None, None]
        slab2_mm = [None, None]
        xn_guard = [None] * 8
        for tt in range(NTT):
            base = 1 + tt * TT
            stat_toks = [None, None]
            for c in range(NCH):
                off = 0 if c >= NCH // 2 else 1
                sl = R[:, c * CW + base - 1 + off: c * CW + base - 1 + off + TT]
                par = c % 2
                gpar = [stat_toks[par]] if stat_toks[par] else []
                tcast = P.ve(lambda e, sl=sl, b=xst[par]: e.tensor_copy(b[:], sl),
                             wait=gpar + ([pool_y_free] if pool_y_free and c == 0
                                          else []))
                tsq = P.se(lambda e, sl=sl, b=sqst[par]: e.activation(
                    b[:], sl, AF.Square),
                    wait=gpar + ([pool_y_free] if pool_y_free and c == 0 else []))
                tmm1 = P.te(lambda e, b=xst[par], c=c: e.matmul(
                    sP, ones_bf[:], b[:], start=(c == 0), stop=(c == NCH - 1)),
                    wait=[tcast] + ([pool_y_free] if pool_y_free and c == 0 else []))
                tmm2 = P.te(lambda e, b=sqst[par], c=c: e.matmul(
                    qP, ones_bf[:], b[:], start=(c == 0), stop=(c == NCH - 1)),
                    wait=[tsq], inc="Ete")
                stat_toks[par] = tmm2
            mm_done = tmm2
            t1 = P.ve(lambda e: e.tensor_scalar_mul(mean, sP, 1.0 / D),
                      wait=[mm_done])
            t2 = P.ve(lambda e: e.tensor_scalar_mul(ex2, qP, 1.0 / D))
            t3 = P.ve(lambda e: e.tensor_mul(var, mean, mean), wait=[t1, t2])
            tv = P.ve(lambda e: e.tensor_sub(var, ex2, var), wait=[t3])
            tln = P.se(lambda e: e.activation(sd, var, AF.Ln, bias=epst[0:1, :]),
                       wait=[tv])
            ts = P.se(lambda e: e.activation(rstd, sd, AF.Exp,
                                             scale=nhalf[0:1, :]), wait=[tln])
            tmr = P.ve(lambda e: e.tensor_mul(mr, mean, rstd), wait=[ts])
            tb1 = P.te(lambda e: e.matmul(rbP, ones_row[:], rstd, start=True,
                                          stop=True), wait=[tmr], inc="Ete")
            tb2 = P.te(lambda e: e.matmul(mbP, ones_row[:], mr, start=True,
                                          stop=True), inc="Ete")
            xn_done = []
            tn3 = None
            for c in range(NCH):
                off = 0 if c >= NCH // 2 else 1
                sl = R[:, c * CW + base - 1 + off: c * CW + base - 1 + off + TT]
                g = [xn_guard[c]] if xn_guard[c] else []
                tn1 = P.ve(lambda e, sl=sl: e.tensor_mul(tmpf[:], sl, rbP),
                           wait=[tb1, tb2] + ([tn3] if tn3 else []))
                tn2 = P.ve(lambda e: e.tensor_sub(tmpf[:], tmpf[:], mbP))
                tn3 = P.se(lambda e, c=c: e.activation(
                    xnst[c][:], tmpf[:], AF.Identity, scale=gcol[:, c:c + 1],
                    bias=bcol[:, c:c + 1]), wait=[tn2] + bt + g)
                xn_done.append(tn3)
            stat_guard = tn3
            tmul = hbuf_guard if tt > 0 else None
            hp_guard = [None, None]
            for oc in range(NOC):
                par = oc % 2
                gm = [slab_mm[par]] if slab_mm[par] else []
                slab_tok[par] = P.dma("gpsimd", slab[par][:],
                                      w1r[:, :, oc * 128:(oc + 1) * 128], wait=gm)
                Hp = pool[:, par * TT:(par + 1) * TT]
                bankw = [slab_tok[par]]
                if oc < 2 and pool_y_free:
                    bankw.append(pool_y_free)
                if hp_guard[par]:
                    bankw.append(hp_guard[par])
                for dc in range(NCH):
                    tk = P.te(lambda e, Hp=Hp, par=par, dc=dc: e.matmul(
                        Hp, slab[par][:, dc * 128:(dc + 1) * 128], xnst[dc][:],
                        start=(dc == 0), stop=(dc == NCH - 1)),
                        wait=(bankw + [xn_done[dc]]) if dc == 0 else [xn_done[dc]],
                        inc="Ete" if dc == NCH - 1 else None)
                slab_mm[par] = tk
                if oc % 2 == 0:
                    ta_ev = P.se(lambda e, Hp=Hp, oc=oc: e.activation(
                        abuf[:], Hp, AF.Identity, bias=bias1[:, oc:oc + 1]),
                        wait=[tk] + ([tmul] if tmul else []), inc="Ese")
                    hp_guard[par] = ta_ev
                else:
                    tg_ev = P.se(lambda e, Hp=Hp, oc=oc: e.activation(
                        gbuf[:], Hp, AF.Gelu, bias=bias1[:, oc:oc + 1]),
                        wait=[tk], inc="Ese")
                    hp_guard[par] = tg_ev
                    tmul = P.ve(lambda e, hcx=oc // 2: e.tensor_mul(
                        hbuf[:, hcx * TT:(hcx + 1) * TT], abuf[:], gbuf[:]),
                        wait=[ta_ev, tg_ev])
            h_done = tmul
            wlast = None
            for hc in range(NHC):
                par = hc % 2
                gm = [slab2_mm[par]] if slab2_mm[par] else []
                slab2_tok[par] = P.dma("gpsimd", slab2[par][:],
                                       w2.ap()[hc * 128:(hc + 1) * 128, :],
                                       wait=gm)
                for dc in range(NCH):
                    tk = P.te(lambda e, par=par, dc=dc, hc=hc: e.matmul(
                        pool[:, dc * TT:(dc + 1) * TT],
                        slab2[par][:, dc * 128:(dc + 1) * 128],
                        hbuf[:, hc * TT:(hc + 1) * TT],
                        start=(hc == 0), stop=(hc == NHC - 1)),
                        wait=[slab2_tok[par], h_done] if dc == 0
                        else [slab2_tok[par]],
                        inc="Ete" if dc == NCH - 1 else None)
                slab2_mm[par] = tk
                wlast = tk
            radds = []
            for c in range(NCH):
                sl = R[:, c * CW + base: c * CW + base + TT]
                radds.append(P.ve(lambda e, sl=sl, c=c: e.scalar_tensor_tensor(
                    sl, pool[:, c * TT:(c + 1) * TT], b2col[:, c:c + 1], sl,
                    ALU.add, ALU.add), wait=[wlast] if c == 0 else []))
            pool_y_free = radds[-1]
            hbuf_guard = wlast
            lastw1 = slab_mm[(NOC - 1) % 2]
            for c in range(NCH):
                xn_guard[c] = lastw1
    P.barrier()


def ph_mla_proj(P):
    R = P.R
    with P.scope():
        wqsb = P.sb("ml_wq", [128, 8 * D], BF16)
        wdsb = P.sb("ml_wd", [128, 8 * 128], BF16)
        wusb = P.sb("ml_wu", [128, 2048], BF16)
        latsb = P.sb("ml_lat", [128, T], BF16)
        xst = [P.sb(f"ml_x{i}", [128, TT], BF16) for i in range(8)]
        ev = [P.sb(f"ml_e{i}", [128, TT], BF16) for i in range(4)]
        pq = [P.ps(f"ml_p{i}", [128, TT]) for i in range(3)]
        wtoks = (_load_w_sbuf(P, wqsb, "mla_wq", D, D)
                 + _load_w_sbuf(P, wdsb, "mla_down", D, 128)
                 + _load_w_sbuf(P, wusb, "mla_up", 128, 2048))
        q_sp, k_sp, v_sp = P.dram["q_sp"], P.dram["k_sp"], P.dram["v_sp"]
        last_mm = [None] * 8
        ev_tok = [None] * 4
        pq_tok = [None] * 3
        pqi = evi = 0

        def step(mmfn, nacc, out_dst, extra_wait=()):
            nonlocal pqi, evi
            pp = pq[pqi % 3]
            gw = [pq_tok[pqi % 3]] if pq_tok[pqi % 3] else []
            for i in range(nacc):
                tk = P.te(lambda e, i=i, pp=pp: mmfn(e, pp, i),
                          wait=(list(extra_wait) + wtoks + gw) if i == 0 else [],
                          inc="Ete" if i == nacc - 1 else None)
            eb = ev[evi % 4]
            ge = [ev_tok[evi % 4]] if ev_tok[evi % 4] else []
            tev = P.se(lambda e, pp=pp, b=eb: e.copy(b[:], pp[:]),
                       wait=[tk] + ge, inc="Ese")
            pq_tok[pqi % 3] = tev
            if out_dst is not None:
                ev_tok[evi % 4] = P.dma("sync", out_dst, eb[:], wait=[tev])
            pqi += 1
            evi += 1
            return tk, tev, eb

        for tt in range(NTT):
            base = 1 + tt * TT
            casts = []
            for c in range(NCH):
                sl = R[:, c * CW + base: c * CW + base + TT]
                g = [last_mm[c]] if last_mm[c] else []
                casts.append(P.ve(lambda e, sl=sl, b=xst[c]: e.tensor_copy(b[:], sl),
                                  wait=g))
            pp = pq[pqi % 3]
            gw = [pq_tok[pqi % 3]] if pq_tok[pqi % 3] else []
            for dc in range(NCH):
                tk = P.te(lambda e, dc=dc, pp=pp: e.matmul(
                    pp[:], _wslice(wdsb, 128, dc, 0), xst[dc][:],
                    start=(dc == 0), stop=(dc == NCH - 1)),
                    wait=([casts[dc]] + wtoks + gw) if dc == 0 else [casts[dc]],
                    inc="Ete" if dc == NCH - 1 else None)
            lat_ev = P.se(lambda e, pp=pp, tt=tt: e.copy(
                latsb[:, tt * TT:(tt + 1) * TT], pp[:]), wait=[tk], inc="Ese")
            pq_tok[pqi % 3] = lat_ev
            pqi += 1
            for dc in range(NCH):
                last_mm[dc] = tk
            for oc in range(8):
                def qmm(e, pp, i, oc=oc):
                    return e.matmul(pp[:], _wslice(wqsb, D, i, oc), xst[i][:],
                                    start=(i == 0), stop=(i == NCH - 1))
                tk, _, _ = step(qmm, NCH,
                                q_sp.ap()[oc * 128:(oc + 1) * 128, tt * TT:(tt + 1) * TT],
                                extra_wait=casts)
                for dc in range(NCH):
                    last_mm[dc] = tk
            for oc in range(8):
                def kmm(e, pp, i, oc=oc, tt=tt):
                    return e.matmul(pp[:], wusb[:, oc * 128:(oc + 1) * 128],
                                    latsb[:, tt * TT:(tt + 1) * TT],
                                    start=True, stop=True)
                step(kmm, 1,
                     k_sp.ap()[oc * 128:(oc + 1) * 128, tt * TT:(tt + 1) * TT],
                     extra_wait=[lat_ev])
            for tc in range(4):
                for half in range(2):
                    def vmm(e, pp, i, tc=tc, half=half, tt=tt):
                        return e.matmul(
                            pp[:], latsb[:, tt * TT + tc * 128: tt * TT + (tc + 1) * 128],
                            wusb[:, 1024 + half * TT: 1024 + (half + 1) * TT],
                            start=True, stop=True)
                    step(vmm, 1,
                         v_sp.ap()[tt * 4 + tc, :, half * TT:(half + 1) * TT],
                         extra_wait=[lat_ev])
    P.barrier()



def ph_moe_precast(P):
    w1 = P.dram["moe_w1"]
    w2 = P.dram["moe_w2"]
    w1bf = P.dram["w1bf"]
    w2bf = P.dram["w2bf"]
    with P.scope():
        slab = [P.sb(f"mc_s{i}", [128, 1024], BF16) for i in range(4)]
        st_tok = [None] * 4
        i = 0
        for e in range(8):
            w1r = w1.ap()[e * D:(e + 1) * D, :].rearrange(
                "(dc p) j -> p dc j", p=128)
            for hc in range(32):
                b = slab[i % 4]
                g = [st_tok[i % 4]] if st_tok[i % 4] else []
                t1 = P.dma("gpsimd", b[:],
                           w1r[:, :, hc * 128:(hc + 1) * 128], wait=g)
                st_tok[i % 4] = P.dma("sync", w1bf.ap()[e, hc], b[:], wait=[t1])
                i += 1
                b = slab[i % 4]
                g = [st_tok[i % 4]] if st_tok[i % 4] else []
                t2 = P.dma("gpsimd", b[:],
                           w2.ap()[e * 4096 + hc * 128: e * 4096 + (hc + 1) * 128, :],
                           wait=g)
                st_tok[i % 4] = P.dma("sync", w2bf.ap()[e, hc], b[:], wait=[t2])
                i += 1
    P.barrier()


def ph_moe(P):
    R = P.R
    ident = P.consts["ident"]
    with P.scope():
        xst = [P.sb(f"mo_x{i}", [128, TT], BF16) for i in range(8)]
        Hsb = P.sb("mo_H", [128, 32 * TT], BF16)
        Ysb = P.sb("mo_Y", [128, 8 * TT], F32)
        w1s = [P.sb(f"mo_w1{i}", [128, 1024], BF16) for i in range(2)]
        w2s = [P.sb(f"mo_w2{i}", [128, 1024], BF16) for i in range(2)]
        cwfm = P.sb("mo_cwfm", [8, TT], BF16)
        cwb_sb = P.sb("mo_cwb", [128, TT], BF16)
        b1col = P.sb("mo_b1c", [128, 32], F32)
        b2sb = P.sb("mo_b2", [8, D], BF16)
        wgsb = P.sb("mo_wg", [128, 8 * 8], BF16)
        tmpf = P.sb("mo_tmp", [128, TT], F32)
        e8 = P.sb("mo_e8", [128, 8], F32)
        k8 = P.sb("mo_k8", [128, 8], F32)
        cw8 = P.sb("mo_cw8", [128, 8], F32)
        cwb8 = P.sb("mo_cwb8", [128, 8], BF16)
        colm = P.sb("mo_colm", [128, 4], F32)
        pool = P.ps("mo_pool", [128, 8 * TT])
        ptr = pool[0:8, 6 * TT + 64: 6 * TT + 128].bitcast(BF16)
        sel8 = P.consts["sel_bcast8"]

        m1neg = colm[:, 0:1]
        m2c = colm[:, 1:2]
        lc = colm[:, 2:3]
        rcp = colm[:, 3:4]

        wg_tok = P.dma("gpsimd", wgsb[:],
                       P.dram["moe_wg"].ap().rearrange("(dc p) j -> p dc j", p=128))
        b2_tok = P.dma("gpsimd", b2sb[:], P.dram["moe_b2"].ap())
        w1bf, w2bf, b1d = P.dram["w1bf"], P.dram["w2bf"], P.dram["moe_b1"]

        cast_guard = None
        yread_guard = None
        slab_tok = [None, None]
        slab2_tok = [None, None]
        slab_mm = [None, None]
        slab2_mm = [None, None]
        pool_free = None
        hsb_guard = None
        cwfm_guard = None
        for tt in range(NTT):
            base = 1 + tt * TT
            casts = []
            for c in range(NCH):
                sl = R[:, c * CW + base: c * CW + base + TT]
                g = [cast_guard] if cast_guard else []
                casts.append(P.ve(lambda e_, sl=sl, b=xst[c]: e_.tensor_copy(b[:], sl),
                                  wait=g))
            Gp = pool[:, 6 * TT: 6 * TT + 32]
            for tc in range(4):
                for dc in range(NCH):
                    tkg = P.te(lambda e_, tc=tc, dc=dc: e_.matmul(
                        Gp[:, tc * 8:(tc + 1) * 8],
                        xst[dc][:, tc * 128:(tc + 1) * 128],
                        wgsb[:, dc * 8:(dc + 1) * 8],
                        start=(dc == 0), stop=(dc == NCH - 1)),
                        wait=[casts[dc], wg_tok, b2_tok]
                        + ([pool_free] if pool_free and dc == 0 else []),
                        inc="Ete" if dc == NCH - 1 else None)
                G = Gp[:, tc * 8:(tc + 1) * 8]
                tm = P.ve(lambda e_, G=G: e_.reduce_max(m1neg, G, axis=AX.X,
                                                        negate=True), wait=[tkg])
                tex = P.se(lambda e_, G=G: e_.activation(
                    e8[:], G, AF.Exp, bias=m1neg, accum_out=lc), wait=[tm])
                teq = P.ve(lambda e_: e_.tensor_single_scalar(
                    k8[:], e8[:], 1.0, ALU.is_ge), wait=[tex])
                tpm = P.ve(lambda e_: e_.scalar_tensor_tensor(
                    cw8[:], k8[:], -1e30, e8[:], ALU.mult, ALU.add), wait=[teq])
                tm2 = P.ve(lambda e_: e_.reduce_max(m2c, cw8[:], axis=AX.X),
                           wait=[tpm])
                tke = P.ve(lambda e_: e_.tensor_single_scalar(
                    k8[:], e8[:], m2c, ALU.is_ge), wait=[tm2])
                trr = P.ve(lambda e_: e_.reciprocal(rcp, lc), wait=[tex])
                tcw = P.ve(lambda e_: e_.tensor_mul(cw8[:], e8[:], k8[:]),
                           wait=[tke, trr])
                tcw2 = P.ve(lambda e_: e_.tensor_scalar_mul(cw8[:], cw8[:], rcp),
                            wait=[tcw])
                tcb = P.ve(lambda e_: e_.tensor_copy(cwb8[:], cw8[:]), wait=[tcw2])
                prevcw = [cw_done] if tc > 0 else (
                    [cwfm_guard] if cwfm_guard else [])
                ttr = P.te(lambda e_: e_.transpose(ptr, cwb8[:], ident[:]),
                           wait=[tcb] + prevcw, inc="Ete")
                tev = P.se(lambda e_, tc=tc: e_.copy(
                    cwfm[:, tc * 128:(tc + 1) * 128], ptr), wait=[ttr])
                cw_done = tev
            yg = [yread_guard] if yread_guard else []
            for dc in range(NCH):
                tb2 = P.te(lambda e_, dc=dc: e_.matmul(
                    pool[:, dc * TT:(dc + 1) * TT],
                    b2sb[:, dc * 128:(dc + 1) * 128], cwfm[:],
                    start=True, stop=True), wait=[cw_done], inc="Ete")
                tev = P.se(lambda e_, dc=dc: e_.copy(
                    Ysb[:, dc * TT:(dc + 1) * TT], pool[:, dc * TT:(dc + 1) * TT]),
                    wait=[tb2] + yg)
                yinit_done = tev
            b1_guard = None
            for e in range(8):
                b1tok = P.dma("sync", b1col[:], b1d.ap()[e],
                              wait=[b1_guard] if b1_guard else [])
                tcwb = P.te(lambda e_, e=e: e_.matmul(
                    pool[:, 6 * TT: 7 * TT], sel8[:, e * 128:(e + 1) * 128],
                    cwfm[:], start=True, stop=True),
                    wait=[cw_done] + ([pool_free] if pool_free else [])
                    + ([yinit_done] if e == 0 else []), inc="Ete")
                tcwe = P.se(lambda e_: e_.copy(cwb_sb[:], pool[:, 6 * TT: 7 * TT]),
                            wait=[tcwb])
                gelu_last = None
                hp_guard = [None, None]
                for hc in range(32):
                    par = hc % 2
                    gm = [slab_mm[par]] if slab_mm[par] else []
                    slab_tok[par] = P.dma(
                        "sync", w1s[par][:], w1bf.ap()[e, hc], wait=gm)
                    Hp = pool[:, par * TT:(par + 1) * TT]
                    bankw = []
                    if hc < 2:
                        if pool_free:
                            bankw.append(pool_free)
                        if e == 0:
                            bankw.append(yinit_done)
                    if hp_guard[par]:
                        bankw.append(hp_guard[par])
                    for dc in range(NCH):
                        tkm = P.te(lambda e_, par=par, dc=dc, Hp=Hp: e_.matmul(
                            Hp, w1s[par][:, dc * 128:(dc + 1) * 128], xst[dc][:],
                            start=(dc == 0), stop=(dc == NCH - 1)),
                            wait=([slab_tok[par]] + bankw) if dc == 0
                            else [slab_tok[par]],
                            inc="Ete" if dc == NCH - 1 else None)
                    slab_mm[par] = tkm
                    gelu_last = P.se(lambda e_, hc=hc, Hp=Hp: e_.activation(
                        Hsb[:, hc * TT:(hc + 1) * TT], Hp, AF.Gelu,
                        bias=b1col[:, hc:hc + 1]), wait=[tkm, b1tok])
                    hp_guard[par] = gelu_last
                b1_guard = gelu_last
                wlast = None
                for hc in range(32):
                    par = hc % 2
                    g = [slab2_tok[par]] if slab2_tok[par] else []
                    gm = [slab2_mm[par]] if slab2_mm[par] else []
                    slab2_tok[par] = P.dma(
                        "sync", w2s[par][:], w2bf.ap()[e, hc], wait=gm)
                    for dc in range(NCH):
                        tkm = P.te(lambda e_, par=par, dc=dc, hc=hc: e_.matmul(
                            pool[:, dc * TT:(dc + 1) * TT],
                            w2s[par][:, dc * 128:(dc + 1) * 128],
                            Hsb[:, hc * TT:(hc + 1) * TT],
                            start=(hc == 0), stop=(hc == 31)),
                            wait=[slab2_tok[par], gelu_last] if dc == 0
                            else [slab2_tok[par]],
                            inc="Ete" if dc == NCH - 1 else None)
                    slab2_mm[par] = tkm
                    wlast = tkm
                hsb_guard = wlast
                for dc in range(NCH):
                    t1 = P.ve(lambda e_, dc=dc: e_.tensor_mul(
                        tmpf[:], pool[:, dc * TT:(dc + 1) * TT], cwb_sb[:]),
                        wait=[wlast, tcwe] if dc == 0 else [])
                    t2 = P.ve(lambda e_, dc=dc: e_.tensor_add(
                        Ysb[:, dc * TT:(dc + 1) * TT],
                        Ysb[:, dc * TT:(dc + 1) * TT], tmpf[:]),
                        wait=[t1] + ([yinit_done] if dc == 0 else []))
                pool_free = t2
            radds = []
            for c in range(NCH):
                sl = R[:, c * CW + base: c * CW + base + TT]
                radds.append(P.ve(lambda e_, sl=sl, c=c: e_.tensor_add(
                    sl, sl, Ysb[:, c * TT:(c + 1) * TT]), wait=[pool_free]))
            yread_guard = radds[-1]
            cast_guard = wlast
            cwfm_guard = tcwb
    P.barrier()



WEIGHT_SPECS = {
    "l0_wq": (D, D), "l0_wk": (D, D), "l0_wv": (D, D), "l0_wo": (D, D),
    "l0_bo": (D, 1),
    "moe_wg": (D, 8), "moe_w1": (8 * D, 4096), "moe_b1": (8, 128, 32),
    "moe_w2": (8 * 4096, D), "moe_b2": (8, D),
    "mla_down": (D, 128), "mla_up": (128, 2048), "mla_wq": (D, D), "mla_wo": (D, D),
}

GATHERED = {"l0_wq", "l0_wk", "l0_wv", "l0_wo", "moe_w1", "moe_w2",
            "mla_down", "mla_up", "mla_wq", "mla_wo"}
for _i in range(6):
    GATHERED |= {f"lf_wqkv_{_i}", f"lf_wo_{_i}", f"lf_w1_{_i}", f"lf_w2_{_i}"}
for _i in range(6):
    WEIGHT_SPECS[f"lf_wqkv_{_i}"] = (D, 3072)
    WEIGHT_SPECS[f"lf_qs_{_i}"] = (64,)
    WEIGHT_SPECS[f"lf_ks_{_i}"] = (64,)
    WEIGHT_SPECS[f"lf_wo_{_i}"] = (D, D)
    WEIGHT_SPECS[f"lf_w1_{_i}"] = (D, 5632)
    WEIGHT_SPECS[f"lf_b1c_{_i}"] = (128, 44)
    WEIGHT_SPECS[f"lf_w2_{_i}"] = (2816, D)
    WEIGHT_SPECS[f"lf_gc_{_i}"] = (128, 8)
    WEIGHT_SPECS[f"lf_bc_{_i}"] = (128, 8)
    WEIGHT_SPECS[f"lf_b2c_{_i}"] = (128, 8)



class LazyDram(dict):

    def __init__(self, nc):
        super().__init__()
        self.nc = nc
        self.used = []
        self.P = None

    def __missing__(self, name):
        if name not in WEIGHT_SPECS:
            raise KeyError(name)
        spec = list(WEIGHT_SPECS[name])
        if name in GATHERED:
            P = self.P
            R, C = spec
            assert R % NCORES == 0, name
            sh = self.nc.dram_tensor(f"{name}_sh", [R // NCORES, C], F32,
                                     kind="ExternalInput")
            bn = self.nc.dram_tensor(f"{name}_bn", [R // NCORES, C], F32,
                                     kind="Internal")
            full = self.nc.dram_tensor(name, [R, C], F32, kind="Internal")
            tok = P.dma("sync", bn.ap(), sh.ap())
            P.gp(lambda e: e.collective_compute(
                "AllGather", ALU.bypass,
                replica_groups=[list(range(NCORES))],
                ins=[bn.ap().opt()], outs=[full.ap().opt()]), wait=[tok])
            P.barrier()
            self[name] = full
            self.used.append(f"{name}_sh")
            return full
        h = self.nc.dram_tensor(name, spec, F32, kind="ExternalInput")
        self[name] = h
        self.used.append(name)
        return h


def build_program(phases, extra_inputs=(), extra_outputs=()):
    stack = contextlib.ExitStack()
    nc = bass.Bass("TRN2", target_bir_lowering=False)
    P = Prog(nc, stack)

    P.dram = LazyDram(nc)
    P.dram.P = P
    P.dram["x_fm"] = nc.dram_tensor("x_fm", [D, T], F32, kind="ExternalInput")
    P.dram["out"] = nc.dram_tensor("out", [D, T], F32, kind="ExternalOutput")
    P.dram["q_sp"] = nc.dram_tensor("q_sp", [D, T], BF16, kind="Internal")
    P.dram["k_sp"] = nc.dram_tensor("k_sp", [D, T], BF16, kind="Internal")
    P.dram["v_sp"] = nc.dram_tensor("v_sp", [32, 128, D], BF16, kind="Internal")
    P.dram["o_sp"] = nc.dram_tensor("o_sp", [D, T], BF16, kind="Internal")
    P.dram["x3_sp"] = nc.dram_tensor("x3_sp", [D, T], F32, kind="Internal")
    P.dram["w1bf"] = nc.dram_tensor("w1bf", [8, 32, 128, 1024], BF16, kind="Internal")
    P.dram["w2bf"] = nc.dram_tensor("w2bf", [8, 32, 128, 1024], BF16, kind="Internal")
    in_names = ["x_fm"]
    for name, shape in extra_inputs:
        P.dram[name] = nc.dram_tensor(name, list(shape), F32, kind="ExternalInput")
        in_names.append(name)
    for name, shape in extra_outputs:
        P.dram[name] = nc.dram_tensor(name, list(shape), F32, kind="ExternalOutput")

    with stack:
        P.R = P.sb("R", [128, NCH * CW], F32)
        P.glob = {"rq_tm": P.sb("rq_tm", [128, 32 * 16], F32)}
        ph_const_load(P)
        for emitter, kw in phases:
            emitter(P, **kw)
        in_names.extend(P.dram.used)
        with (nc.Block() as blk):
            @blk.sync
            def _(e):
                P.emit("sync", e)

            @blk.tensor
            def _(e):
                P.emit("tensor", e)

            @blk.vector
            def _(e):
                P.emit("vector", e)

            @blk.scalar
            def _(e):
                P.emit("scalar", e)

            @blk.gpsimd
            def _(e):
                P.emit("gpsimd", e)
    return nc, in_names


def ph_add_extra(P, src="x3in"):
    xd = P.dram[src]
    R = P.R
    with P.scope():
        xt = P.sb("ax_t", [128, T], F32)
        for c in range(NCH):
            tok = P.dma("sync", xt[:], xd.ap()[c * 128:(c + 1) * 128, :])
            tadd = P.ve(lambda e, c=c: e.tensor_add(
                R[:, c * CW + 1:(c + 1) * CW], R[:, c * CW + 1:(c + 1) * CW],
                xt[:]), wait=[tok])
            P.q("sync", lambda e: e.nop(), wait=[tadd])
    P.barrier()


def _block_phases():
    g1 = [
        (ph_l0_proj, {}),
        (emit_causal_attention, {"rope": True, "scale": 0.125}),
        (ph_out_proj, {"wname": "l0_wo", "bias_name": "l0_bo"}),
        (emit_ln_inplace, {}),
    ]
    g2 = [
        (ph_moe_precast, {}),
        (ph_moe, {}),
        (emit_ln_inplace, {}),
    ]
    g3 = [
        (ph_mla_proj, {}),
        (emit_causal_attention, {"rope": False, "scale": 0.125}),
        (ph_out_proj, {"wname": "mla_wo"}),
        (emit_ln_inplace, {}),
    ]
    g4 = [(ph_lf_save_x3, {})]
    for i in range(6):
        g4 += [
            (ph_lf_qkv, {"i": i}),
            (ph_lf_rope, {"i": i, "which": "q"}),
            (ph_lf_rope, {"i": i, "which": "k"}),
            (ph_lf_attn, {"i": i}),
            (ph_out_proj, {"wname": f"lf_wo_{i}"}),
            (ph_lf_geglu, {"i": i}),
        ]
    g4 += [(ph_final_x3_ln, {})]
    return [g1, g2, g3, g4]


FUSED = False


def _launch_groups():
    gs = []
    gs.append(([
        (ph_l0_proj, {}),
        (emit_causal_attention, {"rope": True, "scale": 0.125}),
        (ph_out_proj, {"wname": "l0_wo", "bias_name": "l0_bo"}),
        (emit_ln_inplace, {}),
    ], ()))
    gs.append(([
        (ph_moe_precast, {}),
        (ph_moe, {}),
        (emit_ln_inplace, {}),
    ], ()))
    gs.append(([
        (ph_mla_proj, {}),
        (emit_causal_attention, {"rope": False, "scale": 0.125}),
        (ph_out_proj, {"wname": "mla_wo"}),
        (emit_ln_inplace, {}),
    ], ()))
    def lf_layer(i):
        return [
            (ph_lf_qkv, {"i": i}),
            (ph_lf_rope, {"i": i, "which": "q"}),
            (ph_lf_rope, {"i": i, "which": "k"}),
            (ph_lf_attn, {"i": i}),
            (ph_out_proj, {"wname": f"lf_wo_{i}"}),
            (ph_lf_geglu, {"i": i}),
        ]
    gs.append((lf_layer(0) + lf_layer(1), ()))
    gs.append((lf_layer(2) + lf_layer(3), ()))
    gs.append((lf_layer(4) + lf_layer(5) + [
        (ph_add_extra, {"src": "x3in"}),
        (emit_ln_inplace, {}),
    ], (("x3in", (D, T)),)))
    return gs


def in_map_for_core(w, in_names, b):
    m = {}
    for k in in_names:
        if k.endswith("_sh") and k[:-3] in w:
            a = w[k[:-3]]
            r = a.shape[0] // NCORES
            m[k] = np.ascontiguousarray(a[b * r:(b + 1) * r])
        elif k in w:
            m[k] = w[k]
    return m


def kernel(**inputs):
    x = inputs["L0_attn_x"]
    w = _prep_weights(inputs)
    cur = [np.ascontiguousarray(x[b].T) for b in range(NCORES)]

    phases = [(ph_load_x, {"src_name": "x_fm"})]
    for g in _block_phases():
        phases += g
    phases += [(ph_store_out, {})]
    nc, in_names = build_program(phases)

    in_maps = []
    for b in range(NCORES):
        m = {"x_fm": cur[b]}
        m.update(in_map_for_core(w, in_names, b))
        in_maps.append(m)

    import time as _time
    t0 = _time.time()
    last = None
    for attempt in range(2):
        try:
            res = run_bass_kernel_spmd(nc, in_maps, core_ids=list(range(NCORES)))
            cur = [np.ascontiguousarray(np.asarray(r["out"]))
                   for r in res.results]
            break
        except Exception as ex:
            last = ex
            print(f"[kernel] launch attempt {attempt} failed: "
                  f"{type(ex).__name__} after {_time.time()-t0:.0f}s", flush=True)
    else:
        raise last
    return np.stack([np.ascontiguousarray(c.T) for c in cur])
